# version 02
# baseline (speedup 1.0000x reference)
import math
import contextlib
import numpy as np

import concourse.bass as bass
import concourse.mybir as mybir
from concourse.alu_op_type import AluOpType as OP
from concourse.bass_utils import run_bass_kernel_spmd

F32 = mybir.dt.float32
BF16 = mybir.dt.bfloat16
I32 = mybir.dt.int32
AF = mybir.ActivationFunctionType
AX = mybir.AxisListType.X


def GG(v, n=2):
    for _ in range(n):
        v.engine_nop()

D = 2048
TO = 1024
TA = 2048
HF = 8192
NC = 16
NTO = 8
NTA = 16
EPS_LN = 1e-5
EPS_RMS = 1e-8
EPS_T = 1e-8
SCALE_ATT = float(np.float32(1.0) / np.float32(np.sqrt(np.float32(128.0))))
BIG = float(np.float32(2.0 ** 100))
C2PI_1 = float(np.float32(2 * math.pi))
C2PI_2 = float(np.float32(2 * math.pi - float(np.float32(2 * math.pi))))
INV2PI = float(np.float32(1.0 / (2 * math.pi)))
PIH_1 = float(np.float32(math.pi / 2))
PIH_2 = float(np.float32(math.pi / 2 - float(np.float32(math.pi / 2))))
PI_F32 = float(np.float32(math.pi))
PI_LO = float(np.nextafter(np.float32(math.pi), np.float32(0.0)))
LVLS = [128.0, 64.0, 32.0, 16.0, 8.0, 4.0, 2.0, 1.0]


class K:
    def __init__(self):
        self.nc = bass.Bass("TRN2", target_bir_lowering=False,
                            detect_race_conditions=False)
        self.es = contextlib.ExitStack()
        self.sems = {}
        self.cnt = {}
        self.xld = [0, 0]

    def sem(self, *names):
        for n in names:
            self.sems[n] = self.es.enter_context(self.nc.semaphore(n))
            self.cnt[n] = 0

    def inc(self, inst, name, k=1):
        inst.then_inc(self.sems[name], k)
        self.cnt[name] += k
        return self.cnt[name]

    def w(self, eng, name, val=None):
        eng.wait_ge(self.sems[name],
                    max(0, self.cnt[name] if val is None else val))

    def sb(self, name, shape, dt):
        return self.es.enter_context(self.nc.sbuf_tensor(name, shape, dt))

    def psum(self, name, shape, dt):
        return self.es.enter_context(self.nc.psum_tensor(name, shape, dt))


class B:
    pass


def _ln_apply_fm(k, v, Bx, i, par, width=D):
    nchunk = NC
    x3 = Bx.xin[par][:, 0:width].rearrange("p (c j) -> p c j", j=128)
    xo3 = Bx.xnb[par][:, 0:width].rearrange("p (c j) -> p c j", j=128)
    nch = width // 128
    mb3 = Bx.m_bc[:, i * 128:(i + 1) * 128].unsqueeze(1).broadcast_to([128, nch, 128])
    rb3 = Bx.r_bc[:, i * 128:(i + 1) * 128].unsqueeze(1).broadcast_to([128, nch, 128])
    v.tensor_tensor(out=xo3, in0=x3, in1=mb3, op=OP.subtract)
    v.tensor_tensor(out=xo3, in0=xo3, in1=rb3, op=OP.mult)


def _rope(k, v, Bx, tile_idx, buf):
    b3 = buf[:].rearrange("p (g c) -> p g c", g=4)
    x0 = b3[:, :, 0:64:2]
    x1 = b3[:, :, 1:64:2]
    co = Bx.cos_sb[:, tile_idx * 32:(tile_idx + 1) * 32].unsqueeze(1).broadcast_to([128, 4, 32])
    si = Bx.sin_sb[:, tile_idx * 32:(tile_idx + 1) * 32].unsqueeze(1).broadcast_to([128, 4, 32])
    o3 = Bx.ohtmp[:].rearrange("p (g c) -> p g c", g=4)
    s0, s1, s2, s3 = o3[:, :, 0:32], o3[:, :, 32:64], o3[:, :, 64:96], o3[:, :, 96:128]
    v.tensor_tensor(out=s0, in0=x0, in1=co, op=OP.mult)
    v.tensor_tensor(out=s1, in0=x1, in1=si, op=OP.mult)
    v.tensor_tensor(out=s2, in0=x0, in1=si, op=OP.mult)
    v.tensor_tensor(out=s3, in0=x1, in1=co, op=OP.mult)
    v.tensor_tensor(out=x0, in0=s0, in1=s1, op=OP.subtract)
    v.tensor_tensor(out=x1, in0=s2, in1=s3, op=OP.add)


def _bcast_row(k, block, Bx, row_ap, dest, ncols):
    nslab = (ncols + 511) // 512

    def te(t):
        k.w(t, "ev")
        for j in range(nslab):
            w0, w1 = j * 512, min(ncols, j * 512 + 512)
            k.inc(t.matmul(Bx.pb[j % 4][:, 0:w1 - w0], Bx.ones1[:],
                           row_ap[0:1, w0:w1], start=True, stop=True), "mm", 1)
    block.tensor(te)

    def sc(s):
        k.w(s, "mm")
        for j in range(nslab):
            w0, w1 = j * 512, min(ncols, j * 512 + 512)
            k.inc(s.copy(out=dest[:, w0:w1], in_=Bx.pb[j % 4][:, 0:w1 - w0]), "ev", 1)
    block.scalar(sc)


def _ladder_emit(k, block, Bx, n_idx, lp, Fd, y_emit, dest_emit, pe0, lad0):
    lz = Bx.lz[lp][:, 0:Fd]
    lr = Bx.lr[lp][:, 0:Fd]
    lb = Bx.lb[lp][:, 0:Fd]

    def vsec(v):
        y_emit(v)
        v.tensor_scalar(out=lz, in0=lz, scalar1=128.5, scalar2=None, op0=OP.add)
        k.inc(v.tensor_copy(out=lr, in_=lz), "pe", 1)
        for j, c in enumerate(LVLS):
            v.wait_ge(k.sems["lad"], max(0, lad0 + n_idx * 8 + j + 1))
            k.inc(v.scalar_tensor_tensor(out=lr, in0=lb, scalar=-c, in1=lr,
                                         op0=OP.mult, op1=OP.add), "pe", 1)
        v.scalar_tensor_tensor(out=lb, in0=lr, scalar=-1.0, in1=lz,
                               op0=OP.mult, op1=OP.add)
        v.tensor_scalar(out=lb, in0=lb, scalar1=-128.0, scalar2=-128.0,
                        op0=OP.add, op1=OP.max)
        dest_emit(v)
        k.inc(v.engine_nop(), "pe", 1)
    block.vector(vsec)

    def ssec(s):
        for j in range(8):
            s.wait_ge(k.sems["pe"], max(0, pe0 + n_idx * 10 + j + 1))
            k.inc(s.activation(out=lb, in_=lr, func=AF.Sigmoid,
                               scale=Bx.big_ap[:], bias=Bx.lvb[j][:]), "lad", 1)
    block.scalar(ssec)


P2PARTS = ("mm", "tr")


NOACC = False


def build(stages="all", p2parts=("mm", "tr"), noacc=False):
    global NOACC
    NOACC = noacc
    global P2PARTS
    P2PARTS = p2parts
    k = K()
    nc = k.nc
    dram = lambda n, s, dt, kind: nc.dram_tensor(n, s, dt, kind=kind).ap()
    B.x_tm = dram("x_tm", [TA, D], F32, "ExternalInput")
    B.x_fm4 = dram("x_fm4", [NTA, NC, 128, 128], F32, "ExternalInput")
    B.wq_d = dram("wq", [D, D], F32, "ExternalInput")
    B.wk_d = dram("wk", [D, 512], F32, "ExternalInput")
    B.wv_d = dram("wv", [D, 512], F32, "ExternalInput")
    B.wo_d = dram("wo", [D, D], F32, "ExternalInput")
    B.w1t_d = dram("w1t", [D, HF], F32, "ExternalInput")
    B.w2t_d = dram("w2t", [D, HF], F32, "ExternalInput")
    B.w3t_d = dram("w3t", [HF, D], F32, "ExternalInput")
    B.g1f_d = dram("g1f", [128, NC], F32, "ExternalInput")
    B.g2f_d = dram("g2f", [128, NC], F32, "ExternalInput")
    B.g1r_d = dram("g1r", [1, D], F32, "ExternalInput")
    B.g2r_d = dram("g2r", [1, D], F32, "ExternalInput")
    B.g3r_d = dram("g3r", [1, HF], F32, "ExternalInput")
    B.pos_d = dram("posf", [128, NTA], I32, "ExternalInput")
    B.h_hbm = dram("h_hbm", [TO, HF], F32,
                   "ExternalOutput" if stages == "alld" else "Internal")
    B.fl_hbm = dram("fl_hbm", [48, 128], F32, "Internal")
    B.out_d = dram("out", [TO, D], F32, "ExternalOutput")
    if stages in ("p3w1d", "p0d", "alld"):
        B.dbg_d = dram("dbg", [1408, 8256], F32, "ExternalOutput")

    k.sem("c", "xe", "xo", "w", "w2", "w3", "mm", "ev", "ro", "tr", "trv",
          "ln", "exp", "av", "avv", "st", "acc", "h", "lad", "pe", "q3",
          "b1", "b2", "b3", "b4", "wm")

    B.ident = k.sb("ident", [128, 128], F32)
    B.identb = k.sb("identb", [128, 128], BF16)
    B.ones1 = k.sb("ones1", [1, 128], F32)
    B.onesc = k.sb("onesc", [128, 1], F32)
    B.g1f = k.sb("g1f_s", [128, NC], F32)
    B.g2f = k.sb("g2f_s", [128, NC], F32)
    B.g3s = k.sb("g3s", [1, 512], F32)
    B.m_tm = k.sb("m_tm", [128, NTA], F32)
    B.var_tm = k.sb("var_tm", [128, NTA], F32)
    B.r_tm = k.sb("r_tm", [128, NTA], F32)
    B.stat48 = k.sb("stat48", [128, 48], F32)
    B.rm1 = k.sb("rm1", [128, NTO], F32)
    B.rm2 = k.sb("rm2", [128, NTO], F32)
    B.isc1 = k.sb("isc1", [128, NTO], F32)
    B.isc2 = k.sb("isc2", [128, NTO], F32)
    B.sig1 = k.sb("sig1", [128, NTO], F32)
    B.sig2 = k.sb("sig2", [128, NTO], F32)
    B.ssq3 = k.sb("ssq3", [128, NTO * 16], F32)
    B.rm3c = k.sb("rm3c", [128, NTO * 16], F32)
    B.gam3 = k.sb("gam3", [128, NTO], F32)
    B.sig3 = k.sb("sig3", [128, NTO], F32)
    B.stsc = k.sb("stsc", [128, 24], F32)
    B.flip48 = k.sb("flip48", [48, 128], F32)
    B.row_s = k.sb("row_s", [1, TA], F32)
    B.big_ap = k.sb("big_ap", [128, 1], F32)
    B.lvb = [k.sb(f"lvb{i}", [128, 1], F32) for i in range(8)]
    B.t_bc = [k.sb(f"t_bc{i}", [128, 1], F32) for i in range(3)]
    B.nt_bc = [k.sb(f"nt_bc{i}", [128, 1], F32) for i in range(3)]
    B.a3_bc = k.sb("a3_bc", [128, 1], F32)
    B.sc11 = k.sb("sc11", [1, 16], F32)
    B.a_sc = k.sb("a_sc", [1, 16], F32)
    B.bn24 = k.sb("bn24", [128, 24], F32)
    B.st2 = k.sb("st2", [128, 2], F32)
    B.nb1 = k.sb("nb1", [128, 1], F32)
    B.io_p = k.sb("io_p", [128, 1], I32)
    B.io_pf = k.sb("io_pf", [128, 1], F32)
    B.pos_s = k.sb("pos_s", [128, NTA], I32)
    B.ohtmp = k.sb("ohtmp", [128, 512], F32)
    B.rcp4 = k.sb("rcp4", [128, 4], F32)
    B.rn4 = k.sb("rn4", [128, 4], F32)

    B.pb = [k.psum(f"pb{i}", [128, 512], F32) for i in range(6)]
    B.pbb = [k.psum(f"pbb{i}", [128, 1024], BF16) for i in range(2)]

    with k.es:
        with nc.Block() as block:
            with contextlib.ExitStack() as scA:
                sba = lambda n, s, dt: scA.enter_context(nc.sbuf_tensor(n, s, dt))
                B.cos_sb = sba("cos_sb", [128, NTA * 32], F32)
                B.sin_sb = sba("sin_sb", [128, NTA * 32], F32)
                B.m_bc = sba("m_bc", [128, TA], F32)
                B.r_bc = sba("r_bc", [128, TA], F32)
                B.xin = [sba(f"xin{i}", [128, D], F32) for i in range(2)]
                B.xnb = [sba(f"xnb{i}", [128, D], F32) for i in range(2)]
                B.ktmp = sba("ktmp", [128, 512], F32)
                with contextlib.ExitStack() as sc0:
                    sb0 = lambda n, s, dt: sc0.enter_context(nc.sbuf_tensor(n, s, dt))
                    B.io_f = sb0("io_f", [128, 128], I32)
                    B.io_ff = sb0("io_ff", [128, 128], F32)
                    B.jrow = sb0("jrow", [1, 32], I32)
                    B.jrowf = sb0("jrowf", [1, 32], F32)
                    B.frow = sb0("frow", [1, 32], F32)
                    B.f_bc = sb0("f_bc", [128, 32], F32)
                    B.posff = sb0("posff", [128, NTA], F32)
                    B.rpk512 = sb0("rpk512", [128, 512], I32)
                    _p0(k, block, B)
                    if stages == "p0d":
                        def p0_dump(sy):
                            k.w(sy, "b3")
                            k.inc(sy.dma_start(out=B.dbg_d[1152:1280, 0:512],
                                               in_=B.cos_sb[:]), "st", 16)
                            k.inc(sy.dma_start(out=B.dbg_d[1280:1408, 0:512],
                                               in_=B.sin_sb[:]), "st", 16)
                            k.inc(sy.dma_start(out=B.dbg_d[0:128, 0:32],
                                               in_=B.f_bc[:]), "st", 16)
                            k.inc(sy.dma_start(out=B.dbg_d[0:128, 32:48],
                                               in_=B.posff[:]), "st", 16)
                            k.inc(sy.dma_start(out=B.dbg_d[128:129, 0:32],
                                               in_=B.frow[:]), "st", 16)
                        block.sync(p0_dump)
                if stages not in ("p0", "p0d"):
                    _p1(k, block, B)
                with contextlib.ExitStack() as scKV:
                    sbk = lambda n, s, dt: scKV.enter_context(nc.sbuf_tensor(n, s, dt))
                    B.k_fm = sbk("k_fm", [128, 4 * TA], F32)
                    B.v_sb = sbk("v_sb", [128, NTA * 516], F32)
                    if stages not in ("p0", "p0d", "p1"):
                        _p2(k, block, B)
                    if stages in ("p0", "p0d", "p1", "p2"):
                        raise_skip = True
                    with contextlib.ExitStack() as sc3:
                        sb3 = lambda n, s, dt: sc3.enter_context(nc.sbuf_tensor(n, s, dt))
                        B.wqo = sb3("wqo", [128, NC * 512], F32)
                        B.ptile = [sb3(f"ptile{i}", [128, 512], F32) for i in range(2)]
                        B.q_fm = sb3("q_fm", [128, 4 * TO], F32)
                        B.oh_fm = sb3("oh_fm", [128, 4 * TO], F32)
                        B.wos = sb3("wos", [128, 4 * 256], F32)
                        if stages in ("all", "alld", "p3", "p5", "p3q", "p3qa", "p3a1", "p3a2", "p3a4", "p3w1", "p3w2", "p3w2q", "p3w2a", "p3w2a1", "p3w3", "p3w4q", "p3w4a", "p3w1d"):
                            for Q in range(4):
                                _p3_qproj(k, block, B, Q)
                                if stages == "p3w2q" and Q == 1:
                                    break
                                if stages == "p3w4q" and Q == 3:
                                    break
                                if stages == "p3q":
                                    break
                                if stages == "p3a1":
                                    _p3_attn(k, block, B, Q, 1, 1)
                                    break
                                if stages == "p3a2":
                                    _p3_attn(k, block, B, Q, 1, 2)
                                    break
                                if stages == "p3w2a1":
                                    if Q == 0:
                                        _p3_attn(k, block, B, Q)
                                        _p3_wo(k, block, B, Q)
                                        continue
                                    _p3_attn(k, block, B, Q, 1, 1)
                                    break
                                if stages == "p3a4":
                                    _p3_attn(k, block, B, Q, 2, 2)
                                    break
                                _p3_attn(k, block, B, Q)
                                if stages == "p3qa":
                                    break
                                if stages == "p3w1d" and Q == 0:
                                    def dbg_dump(sy):
                                        k.w(sy, "trv")
                                        k.inc(sy.dma_start(out=B.dbg_d[0:128, 0:8192],
                                                           in_=B.k_fm[:]), "st", 16)
                                        k.inc(sy.dma_start(out=B.dbg_d[128:256, 0:4096],
                                                           in_=B.q_fm[:]), "st", 16)
                                        k.inc(sy.dma_start(out=B.dbg_d[256:384, 0:8256],
                                                           in_=B.v_sb[:]), "st", 16)
                                        k.inc(sy.dma_start(out=B.dbg_d[384:512, 0:4096],
                                                           in_=B.oh_fm[:]), "st", 16)
                                        k.inc(sy.dma_start(out=B.dbg_d[512:640, 0:2048],
                                                           in_=B.m_bc[:]), "st", 16)
                                        k.inc(sy.dma_start(out=B.dbg_d[640:768, 0:2048],
                                                           in_=B.r_bc[:]), "st", 16)
                                        k.inc(sy.dma_start(out=B.dbg_d[768:816, 0:128],
                                                           in_=B.flip48[:]), "st", 16)
                                        k.inc(sy.dma_start(out=B.dbg_d[816:944, 0:48],
                                                           in_=B.stat48[:]), "st", 16)
                                        k.inc(sy.dma_start(out=B.dbg_d[944:1072, 0:16],
                                                           in_=B.m_tm[:]), "st", 16)
                                        k.inc(sy.dma_start(out=B.dbg_d[944:1072, 16:32],
                                                           in_=B.var_tm[:]), "st", 16)
                                        k.inc(sy.dma_start(out=B.dbg_d[944:1072, 32:48],
                                                           in_=B.r_tm[:]), "st", 16)
                                        k.inc(sy.dma_start(out=B.dbg_d[1152:1280, 0:512],
                                                           in_=B.cos_sb[:]), "st", 16)
                                        k.inc(sy.dma_start(out=B.dbg_d[1280:1408, 0:512],
                                                           in_=B.sin_sb[:]), "st", 16)
                                    block.sync(dbg_dump)
                                if stages == "p3w2a" and Q == 1:
                                    break
                                if stages == "p3w4a" and Q == 3:
                                    break
                                _p3_wo(k, block, B, Q)
                                if stages in ("p3w1", "p3w1d"):
                                    break
                                if stages == "p3w2" and Q == 1:
                                    break
                                if stages == "p3w3" and Q == 2:
                                    break
                if stages in ("all", "alld", "p5"):
                    _p5a(k, block, B)
                    with contextlib.ExitStack() as scBC:
                        sbb = lambda n, s, dt: scBC.enter_context(nc.sbuf_tensor(n, s, dt))
                        B.k1 = sbb("k1_fm", [128, NC * TO], BF16)
                        B.k2 = sbb("k2_fm", [128, NC * TO], BF16)
                        with contextlib.ExitStack() as scL1:
                            sbl = lambda n, s, dt: scL1.enter_context(nc.sbuf_tensor(n, s, dt))
                            B.lz = [sbl(f"lz{i}", [128, 1024], F32) for i in range(2)]
                            B.lr = [sbl(f"lr{i}", [128, 1024], F32) for i in range(2)]
                            B.lb = [sbl(f"lb{i}", [128, 1024], F32) for i in range(2)]
                            _p5b(k, block, B)
                        _p5c(k, block, B)
            if stages in ("all", "alld", "p5"):
                _p5d(k, block, B)
                with contextlib.ExitStack() as scE:
                    sbe = lambda n, s, dt: scE.enter_context(nc.sbuf_tensor(n, s, dt))
                    B.k3 = sbe("k3_fm", [128, 64 * TO], BF16)
                    with contextlib.ExitStack() as scE2:
                        sb2 = lambda n, s, dt: scE2.enter_context(nc.sbuf_tensor(n, s, dt))
                        B.lz = [sb2(f"lz3{i}", [128, 512], F32) for i in range(2)]
                        B.lr = [sb2(f"lr3{i}", [128, 512], F32) for i in range(2)]
                        B.lb = [sb2(f"lb3{i}", [128, 512], F32) for i in range(2)]
                        B.g3bc2 = sb2("g3bc2", [128, 512], F32)
                        B.hld = [sb2(f"hld{i}", [128, 512], F32) for i in range(2)]
                        B.k3t = sb2("k3t", [128, 512], BF16)
                        _p5e(k, block, B)
                    _p5f(k, block, B)

            if stages == "alld":
                def p5_dump(sy):
                    for sem in ("acc", "ev", "b4", "trv", "av"):
                        k.w(sy, sem)
                    dmp = [(B.a_sc[0:1, 0:16], 0, 0, 1, 16),
                           (B.sc11[0:1, 0:16], 1, 0, 1, 16),
                           (B.sig1[:], 2, 0, 128, NTO),
                           (B.sig2[:], 2, 16, 128, NTO),
                           (B.isc1[:], 2, 32, 128, NTO),
                           (B.isc2[:], 2, 48, 128, NTO),
                           (B.gam3[:], 2, 64, 128, NTO),
                           (B.sig3[:], 2, 80, 128, NTO),
                           (B.stat48[:], 2, 96, 128, 48),
                           (B.ssq3[:], 2, 144, 128, NTO * 16),
                           (B.rm3c[:], 2, 272, 128, NTO * 16),
                           (B.t_bc[2][:], 2, 400, 128, 1),
                           (B.nt_bc[2][:], 2, 401, 128, 1),
                           (B.a3_bc[:], 2, 402, 128, 1)]
                    for src, row, col, nr, ncol in dmp:
                        k.inc(sy.dma_start(
                            out=B.dbg_d[row * 128:row * 128 + nr, col:col + ncol],
                            in_=src, allow_slow_non_contiguous=True), "st", 16)
                block.sync(p5_dump)

            n_acc, n_st, n_h = k.cnt["acc"], k.cnt["st"], k.cnt["h"]

            def gfin(g):
                g.wait_ge(k.sems["acc"], max(0, n_acc))
            block.gpsimd(gfin)

            def syfin(sy):
                sy.wait_ge(k.sems["st"], max(0, n_st))
                sy.wait_ge(k.sems["h"], max(0, n_h))
                sy.wait_ge(k.sems["acc"], max(0, n_acc))
            block.sync(syfin)
    return nc


def _p0(k, block, Bx):
    def sy(sy):
        k.inc(sy.dma_start(out=Bx.g1f[:], in_=Bx.g1f_d), "c", 16)
        k.inc(sy.dma_start(out=Bx.g2f[:], in_=Bx.g2f_d), "c", 16)
        k.inc(sy.dma_start(out=Bx.pos_s[:], in_=Bx.pos_d), "c", 16)
    block.sync(sy)
    n_c0 = k.cnt["c"]

    def gp(g):
        k.inc(g.iota(out=Bx.io_p[:], pattern=[[1, 1]], base=0, channel_multiplier=1),
              "b2", 1)
        k.inc(g.iota(out=Bx.io_f[:], pattern=[[1, 128]], base=0, channel_multiplier=0),
              "b2", 1)
        k.inc(g.iota(out=Bx.jrow[:], pattern=[[1, 32]], base=0, channel_multiplier=0),
              "b2", 1)
    block.gpsimd(gp)

    def vid(v):
        k.w(v, "b2", 3)
        v.tensor_copy(out=Bx.io_pf[:], in_=Bx.io_p[:])
        v.tensor_copy(out=Bx.io_ff[:], in_=Bx.io_f[:])
        GG(v)
        v.tensor_scalar(out=Bx.ident[:], in0=Bx.io_ff[:], scalar1=Bx.io_pf[:],
                        scalar2=None, op0=OP.is_equal)
        k.inc(v.tensor_scalar(out=Bx.identb[:], in0=Bx.io_ff[:], scalar1=Bx.io_pf[:],
                              scalar2=None, op0=OP.is_equal), "b1", 1)
    block.vector(vid)

    def ve(v):
        v.memset(Bx.ones1[:], 1.0)
        v.memset(Bx.onesc[:], 1.0)
        v.memset(Bx.big_ap[:], BIG)
        for i, c in enumerate(LVLS):
            v.memset(Bx.lvb[i][:], -BIG * c)
        k.w(v, "c", n_c0)
        v.tensor_copy(out=Bx.jrowf[:], in_=Bx.jrow[:])
        k.inc(v.tensor_copy(out=Bx.posff[:], in_=Bx.pos_s[:]), "b1", 1)
    block.vector(ve)

    def vc10k(v):
        k.inc(v.memset(Bx.sc11[0:1, 0:1], 10000.0), "b1", 1)
    block.vector(vc10k)

    def sc(s):
        k.w(s, "b1", 3)
        k.inc(s.activation(out=Bx.sc11[0:1, 1:2], in_=Bx.sc11[0:1, 0:1], func=AF.Ln),
              "b2", 1)
    block.scalar(sc)

    def ve2(v):
        k.w(v, "b2")
        k.inc(v.tensor_scalar(out=Bx.jrowf[:], in0=Bx.jrowf[:],
                              scalar1=Bx.sc11[0:1, 1:2], scalar2=-1.0 / 32.0,
                              op0=OP.mult, op1=OP.mult), "b2", 1)
    block.vector(ve2)

    def sc2(s):
        k.w(s, "b2")
        k.inc(s.activation(out=Bx.frow[:], in_=Bx.jrowf[:], func=AF.Exp), "b2", 1)
    block.scalar(sc2)

    def te(t):
        k.w(t, "b2")
        k.inc(t.matmul(Bx.pb[5][:, 0:32], Bx.ones1[:], Bx.frow[:], start=True,
                       stop=True), "mm", 1)
    block.tensor(te)

    def sc3(s):
        k.w(s, "mm")
        k.inc(s.copy(out=Bx.f_bc[:], in_=Bx.pb[5][:, 0:32]), "b2", 1)
    block.scalar(sc3)
    n_b2 = k.cnt["b2"]

    def ve3(v):
        k.w(v, "b2")

        TH = Bx.xnb[0][:, 0:512]
        Y = Bx.xnb[0][:, 512:1024]
        KF = Bx.xnb[0][:, 1024:1536]
        FLG = Bx.xnb[0][:, 1536:2048]
        f_b = Bx.f_bc[:].unsqueeze(1).broadcast_to([128, NTA, 32])
        p_b = Bx.posff[:].unsqueeze(2).broadcast_to([128, NTA, 32])

        def wrap_clamp(dst):
            v.tensor_scalar(out=FLG, in0=Y, scalar1=PI_F32,
                            scalar2=None, op0=OP.is_gt)
            v.scalar_tensor_tensor(out=FLG, in0=Y, scalar=-PI_F32,
                                   in1=FLG, op0=OP.is_lt, op1=OP.subtract)
            v.scalar_tensor_tensor(out=Y, in0=FLG, scalar=C2PI_1,
                                   in1=Y, op0=OP.mult, op1=OP.add)
            v.scalar_tensor_tensor(out=Y, in0=FLG, scalar=C2PI_2,
                                   in1=Y, op0=OP.mult, op1=OP.add)
            v.tensor_scalar(out=dst, in0=Y, scalar1=PI_LO, scalar2=-PI_LO,
                            op0=OP.min, op1=OP.max)

        v.tensor_tensor(out=TH.rearrange("p (t j) -> p t j", j=32), in0=f_b,
                        in1=p_b, op=OP.mult)
        v.tensor_scalar(out=Y, in0=TH, scalar1=INV2PI, scalar2=None, op0=OP.mult)
        v.tensor_copy(out=Bx.rpk512[:], in_=Y)
        v.tensor_copy(out=KF, in_=Bx.rpk512[:])
        v.scalar_tensor_tensor(out=Y, in0=KF, scalar=-C2PI_1,
                               in1=TH, op0=OP.mult, op1=OP.add)
        v.scalar_tensor_tensor(out=Y, in0=KF, scalar=-C2PI_2,
                               in1=Y, op0=OP.mult, op1=OP.add)
        wrap_clamp(Bx.sin_sb[:])
        v.tensor_scalar(out=Y, in0=TH, scalar1=INV2PI, scalar2=0.25,
                        op0=OP.mult, op1=OP.add)
        v.tensor_copy(out=Bx.rpk512[:], in_=Y)
        v.tensor_copy(out=KF, in_=Bx.rpk512[:])
        v.scalar_tensor_tensor(out=Y, in0=KF, scalar=-C2PI_1,
                               in1=TH, op0=OP.mult, op1=OP.add)
        v.scalar_tensor_tensor(out=Y, in0=KF, scalar=-C2PI_2,
                               in1=Y, op0=OP.mult, op1=OP.add)
        v.tensor_scalar(out=Y, in0=Y,
                        scalar1=PIH_1, scalar2=PIH_2, op0=OP.add, op1=OP.add)
        wrap_clamp(Bx.cos_sb[:])
        k.inc(v.engine_nop(), "b3", 1)
    block.vector(ve3)

    def sc4(s):
        k.w(s, "b3", 1)
        k.inc(s.activation(out=Bx.sin_sb[:], in_=Bx.sin_sb[:], func=AF.Sin), "b3", 1)
        k.inc(s.activation(out=Bx.cos_sb[:], in_=Bx.cos_sb[:], func=AF.Sin), "b3", 1)
    block.scalar(sc4)


def _p1(k, block, Bx):
    nc = k.nc
    with contextlib.ExitStack() as es1:
        g1bc = es1.enter_context(nc.sbuf_tensor("g1bc", [128, D], F32))
        g2bc = es1.enter_context(nc.sbuf_tensor("g2bc", [128, D], F32))

        def syg1(sy):
            k.inc(sy.dma_start(out=Bx.row_s[:, 0:D], in_=Bx.g1r_d), "w", 16)
        block.sync(syg1)
        n_wg1 = k.cnt["w"]

        def tw(t):
            k.w(t, "c")
            k.w(t, "b3")
            k.w(t, "b1")
            k.w(t, "w", n_wg1)
        block.tensor(tw)
        _bcast_row(k, block, Bx, Bx.row_s, g1bc, D)
        n_mm_g1 = k.cnt["mm"]

        def syg2(sy):
            k.w(sy, "mm", n_mm_g1)
            k.inc(sy.dma_start(out=Bx.row_s[:, 0:D], in_=Bx.g2r_d), "w", 16)
        block.sync(syg2)
        n_wg2 = k.cnt["w"]

        def tw2a(t):
            k.w(t, "w", n_wg2)
        block.tensor(tw2a)
        _bcast_row(k, block, Bx, Bx.row_s, g2bc, D)
        n_ev_g = k.cnt["ev"]

        ln0 = k.cnt["ln"]
        for i in range(NTA):
            par = i % 2
            pn = "xe" if par == 0 else "xo"

            def syd(sy, i=i, par=par, pn=pn):
                if i >= 2:
                    k.w(sy, "ln", ln0 + i - 1)
                k.xld[par] += 1
                k.inc(sy.dma_start(out=Bx.xin[par][:],
                                   in_=Bx.x_tm[i * 128:(i + 1) * 128, :]), pn, 16)
            block.sync(syd)

            def ve(v, i=i, par=par, pn=pn):
                if i == 0:
                    k.w(v, "ev", n_ev_g)
                v.wait_ge(k.sems[pn], 16 * k.xld[par])
                xt = Bx.xin[par]
                for q in range(4):
                    v.bn_stats(out=Bx.bn24[:, q * 6:(q + 1) * 6],
                               in_=xt[:, q * 512:(q + 1) * 512])
                GG(v)
                v.bn_aggr(out=Bx.st2[:], in_=Bx.bn24[:])
                GG(v)
                v.tensor_copy(out=Bx.m_tm[:, i:i + 1], in_=Bx.st2[:, 0:1])
                v.tensor_copy(out=Bx.var_tm[:, i:i + 1], in_=Bx.st2[:, 1:2])
                k.inc(v.tensor_scalar(out=Bx.stsc[:, 0:1], in0=Bx.st2[:, 1:2],
                                      scalar1=EPS_LN, scalar2=None, op0=OP.add),
                      "b4", 1)
            block.vector(ve)

            def sq(s, i=i):
                k.w(s, "b4")
                k.inc(s.activation(out=Bx.stsc[:, 1:2], in_=Bx.stsc[:, 0:1],
                                   func=AF.Sqrt), "b4", 1)
            block.scalar(sq)

            def vn(v, i=i):
                k.w(v, "b4")
                v.reciprocal(out=Bx.stsc[:, 2:3], in_=Bx.stsc[:, 1:2])
                GG(v)
                v.tensor_mul(out=Bx.stsc[:, 3:4], in0=Bx.stsc[:, 2:3],
                             in1=Bx.stsc[:, 2:3])
                GG(v)
                v.tensor_mul(out=Bx.stsc[:, 3:4], in0=Bx.stsc[:, 3:4],
                             in1=Bx.stsc[:, 0:1])
                GG(v)
                v.tensor_scalar(out=Bx.stsc[:, 3:4], in0=Bx.stsc[:, 3:4],
                                scalar1=-0.5, scalar2=1.5, op0=OP.mult, op1=OP.add)
                GG(v)
                v.tensor_mul(out=Bx.r_tm[:, i:i + 1], in0=Bx.stsc[:, 2:3],
                             in1=Bx.stsc[:, 3:4])
                if i < NTO:
                    GG(v)
                    v.tensor_mul(out=Bx.nb1[:], in0=Bx.m_tm[:, i:i + 1],
                                 in1=Bx.r_tm[:, i:i + 1])
                    GG(v)
                    k.inc(v.tensor_scalar(out=Bx.nb1[:], in0=Bx.nb1[:], scalar1=-1.0,
                                          scalar2=None, op0=OP.mult), "ro", 1)
                else:
                    k.inc(v.engine_nop(), "ln", 1)
            block.vector(vn)

            if i < NTO:
                def sl(s, i=i, par=par):
                    k.w(s, "ro")
                    k.inc(s.activation(out=Bx.xnb[0][:], in_=Bx.xin[par][:],
                                       func=AF.Identity, bias=Bx.nb1[:],
                                       scale=Bx.r_tm[:, i:i + 1]), "exp", 1)
                block.scalar(sl)

                def vr(v, i=i):
                    k.w(v, "exp")
                    v.tensor_mul(out=Bx.xnb[1][:], in0=Bx.xnb[0][:], in1=g1bc[:])
                    v.reduce_max(out=Bx.rm1[:, i:i + 1], in_=Bx.xnb[1][:], axis=AX,
                                 apply_absolute_value=True)
                    v.tensor_mul(out=Bx.xnb[1][:], in0=Bx.xnb[0][:], in1=g2bc[:])
                    v.reduce_max(out=Bx.rm2[:, i:i + 1], in_=Bx.xnb[1][:], axis=AX,
                                 apply_absolute_value=True)
                    k.inc(v.engine_nop(), "ln", 1)
                block.vector(vr)
        n_ln1 = k.cnt["ln"]

        def vt(v):
            k.w(v, "ln", n_ln1)
            ms = Bx.stsc[:, 8:16]
            v.tensor_mul(out=ms[:], in0=Bx.r_tm[:, 0:NTO], in1=Bx.r_tm[:, 0:NTO])
            GG(v)
            v.tensor_mul(out=ms[:], in0=ms[:], in1=Bx.var_tm[:, 0:NTO])
            GG(v)
            k.inc(v.tensor_scalar(out=ms[:], in0=ms[:], scalar1=EPS_RMS, scalar2=None,
                                  op0=OP.add), "b4", 1)
        block.vector(vt)

        def st(s):
            k.w(s, "b4")
            k.inc(s.activation(out=Bx.stsc[:, 16:24], in_=Bx.stsc[:, 8:16],
                               func=AF.Sqrt), "b4", 1)
        block.scalar(st)

        def vt2(v):
            k.w(v, "b4")
            ms = Bx.stsc[:, 8:16]
            rr = Bx.stat48[:, 0:8]
            t1 = Bx.stat48[:, 8:16]
            v.reciprocal(out=rr[:], in_=Bx.stsc[:, 16:24])
            GG(v)
            v.tensor_mul(out=t1[:], in0=rr[:], in1=rr[:])
            GG(v)
            v.tensor_mul(out=t1[:], in0=t1[:], in1=ms[:])
            GG(v)
            v.tensor_scalar(out=t1[:], in0=t1[:], scalar1=-0.5, scalar2=1.5,
                            op0=OP.mult, op1=OP.add)
            GG(v)
            v.tensor_mul(out=rr[:], in0=rr[:], in1=t1[:])
            GG(v)
            q8 = Bx.stat48[:, 16:24]
            p8 = Bx.stat48[:, 24:32]
            for rm, iscx, gidx in ((Bx.rm1, Bx.isc1, 32), (Bx.rm2, Bx.isc2, 40)):
                v.tensor_mul(out=t1[:], in0=rr[:], in1=rm[:, 0:NTO])
                GG(v)
                v.tensor_scalar(out=t1[:], in0=t1[:], scalar1=1e-4, scalar2=None,
                                op0=OP.max)
                GG(v)
                v.tensor_scalar(out=iscx[:], in0=t1[:], scalar1=1.0 / 127.0,
                                scalar2=None, op0=OP.mult)
                GG(v)
                v.reciprocal(out=q8[:], in_=t1[:])
                GG(v)
                v.tensor_mul(out=p8[:], in0=q8[:], in1=t1[:])
                GG(v)
                v.tensor_scalar(out=p8[:], in0=p8[:], scalar1=-1.0, scalar2=2.0,
                                op0=OP.mult, op1=OP.add)
                GG(v)
                v.tensor_mul(out=q8[:], in0=q8[:], in1=p8[:])
                GG(v)
                v.tensor_mul(out=q8[:], in0=q8[:], in1=rr[:])
                GG(v)
                v.tensor_scalar(out=Bx.stat48[:, gidx:gidx + 8], in0=q8[:],
                                scalar1=127.0, scalar2=None, op0=OP.mult)
                GG(v)
            v.tensor_copy(out=Bx.stat48[:, 0:16], in_=Bx.m_tm[:])
            k.inc(v.tensor_copy(out=Bx.stat48[:, 16:32], in_=Bx.r_tm[:]), "b3", 1)
        block.vector(vt2)

        def tf(t):
            k.w(t, "b3")
            k.w(t, "b1", 1)
            k.w(t, "ev")
            k.inc(t.transpose(Bx.pb[0][0:48, 0:128], Bx.stat48[:], Bx.ident[:]),
                  "tr", 1)
        block.tensor(tf)

        def sf(s):
            k.w(s, "tr")
            k.inc(s.copy(out=Bx.flip48[:], in_=Bx.pb[0][0:48, 0:128]), "ev", 1)
        block.scalar(sf)
        n_ev_fl = k.cnt["ev"]

        fl = Bx.fl_hbm.rearrange("a b -> (a b)")

        def syf(sy):
            k.w(sy, "ev", n_ev_fl)
            k.inc(sy.dma_start(out=Bx.fl_hbm, in_=Bx.flip48[:]), "st", 16)
            k.w(sy, "st")
            k.inc(sy.dma_start(out=Bx.row_s[:],
                               in_=fl[0:2048].rearrange("(o x) -> o x", o=1)), "w", 16)
        block.sync(syf)
        n_w_m = k.cnt["w"]

        def twm(t):
            k.w(t, "w", n_w_m)
        block.tensor(twm)
        _bcast_row(k, block, Bx, Bx.row_s, Bx.m_bc, TA)
        n_mm_m = k.cnt["mm"]

        def syr(sy):
            k.w(sy, "mm", n_mm_m)
            k.inc(sy.dma_start(out=Bx.row_s[:],
                               in_=fl[2048:4096].rearrange("(o x) -> o x", o=1)), "w", 16)
        block.sync(syr)
        n_w_r = k.cnt["w"]

        def twr(t):
            k.w(t, "w", n_w_r)
        block.tensor(twr)
        _bcast_row(k, block, Bx, Bx.row_s, Bx.r_bc, TA)
        Bx.n_ev_p1 = k.cnt["ev"]
        Bx.n_ln_gdone = ln0 + 8


def _p2(k, block, Bx):
    nc = k.nc
    with contextlib.ExitStack() as es2:
        wk_sb = es2.enter_context(nc.sbuf_tensor("wk_sb", [128, NC * 512], F32))
        wv_sb = es2.enter_context(nc.sbuf_tensor("wv_sb", [128, NC * 512], F32))

        def syw(sy):
            sy.wait_ge(k.sems["ln"], max(0, Bx.n_ln_gdone))
            k.inc(sy.dma_start(out=wk_sb[:].rearrange("p (c n) -> p c n", c=NC),
                               in_=Bx.wk_d.rearrange("(c p) n -> p c n", p=128)),
                  "w2", 16)
            k.inc(sy.dma_start(out=wv_sb[:].rearrange("p (c n) -> p c n", c=NC),
                               in_=Bx.wv_d.rearrange("(c p) n -> p c n", p=128)),
                  "w2", 16)
        block.sync(syw)
        n_wkv = k.cnt["w2"]

        def vones(v):
            vv = Bx.v_sb[:].rearrange("p (t g c) -> p t g c", t=NTA, g=4)
            k.inc(v.memset(vv[:, :, :, 128:129], 1.0), "b1", 1)
        block.vector(vones)

        ln0 = k.cnt["ln"]
        mm0 = k.cnt["mm"]
        ev0 = k.cnt["ev"]
        ro0 = k.cnt["ro"]
        tr0 = k.cnt["tr"]
        kf4 = Bx.k_fm[:].rearrange("p (g ta) -> p g ta", g=4)
        vv4 = Bx.v_sb[:].rearrange("p (t g c) -> p t g c", t=NTA, g=4)

        for i in range(NTA):
            par = i % 2
            pn = "xe" if par == 0 else "xo"

            def syd(sy, i=i, par=par, pn=pn):
                if i >= 2:
                    k.w(sy, "ln", ln0 + i - 1)
                k.xld[par] += 1
                k.inc(sy.dma_start(out=Bx.xin[par][:].rearrange("p (c j) -> p c j", c=NC),
                                   in_=Bx.x_fm4[i].rearrange("c p j -> p c j")), pn, 16)
            block.sync(syd)

            def ve(v, i=i, par=par, pn=pn):
                v.wait_ge(k.sems[pn], 16 * k.xld[par])
                if i < 2:
                    v.wait_ge(k.sems["ev"], max(0, Bx.n_ev_p1))
                if i >= 2:
                    v.wait_ge(k.sems["mm"], max(0, mm0 + 2 * (i - 1)))
                _ln_apply_fm(k, v, Bx, i, par)
                k.inc(v.engine_nop(), "ln", 1)
            block.vector(ve)

            def te(t, i=i, par=par):
                if i == 0:
                    k.w(t, "w2", n_wkv)
                    k.w(t, "ev")
                k.w(t, "ln", ln0 + i + 1)
                if i >= 1:
                    t.wait_ge(k.sems["ev"], max(0, ev0 + 3 * (i - 1)) + 1)
                xt = Bx.xnb[par]
                for c in range(NC):
                    inst = t.matmul(Bx.pb[0][:, 0:512], xt[:, c * 128:(c + 1) * 128],
                                    wk_sb[:, c * 512:(c + 1) * 512],
                                    start=(c == 0), stop=(c == NC - 1))
                k.inc(inst, "mm", 1)
                if i >= 1:
                    t.wait_ge(k.sems["ev"], max(0, ev0 + 3 * (i - 1)) + 2)
                for c in range(NC):
                    inst = t.matmul(Bx.pb[1][:, 0:512], xt[:, c * 128:(c + 1) * 128],
                                    wv_sb[:, c * 512:(c + 1) * 512],
                                    start=(c == 0), stop=(c == NC - 1))
                k.inc(inst, "mm", 1)
                if i >= 1:
                    t.wait_ge(k.sems["ro"], max(0, ro0 + i))
                    if i >= 3:
                        t.wait_ge(k.sems["ev"], max(0, ev0 + 3 * (i - 2)) + 3)
                    for b4i in range(4):
                        inst = t.transpose(Bx.pb[2 + ((i - 1) % 2)][:, b4i * 128:(b4i + 1) * 128],
                                           Bx.ktmp[:, b4i * 128:(b4i + 1) * 128], Bx.ident[:])
                    k.inc(inst, "tr", 1)
            block.tensor(te)

            def se(s, i=i):
                s.wait_ge(k.sems["mm"], max(0, mm0 + 2 * i + 1))
                if i >= 1:
                    s.wait_ge(k.sems["tr"], max(0, tr0 + i))
                k.inc(s.copy(out=Bx.ktmp[:], in_=Bx.pb[0][:, 0:512]), "ev", 1)
                s.wait_ge(k.sems["mm"], max(0, mm0 + 2 * i + 2))
                k.inc(s.copy(out=vv4[:, i, :, 0:128],
                             in_=Bx.pb[1][:, 0:512].rearrange("p (g c) -> p g c", g=4)),
                      "ev", 1)
                if i >= 1:
                    k.inc(s.copy(out=kf4[:, :, (i - 1) * 128:i * 128],
                                 in_=Bx.pb[2 + ((i - 1) % 2)][:, 0:512]
                                 .rearrange("p (g c) -> p g c", g=4)), "ev", 1)
                else:
                    k.inc(s.copy(out=Bx.sc11[0:1, 15:16], in_=Bx.ones1[0:1, 0:1]),
                          "ev", 1)
            block.scalar(se)

            def vr(v, i=i):
                v.wait_ge(k.sems["ev"], max(0, ev0 + 3 * i + 1))
                _rope(k, v, Bx, i, Bx.ktmp)
                k.inc(v.engine_nop(), "ro", 1)
            block.vector(vr)

        def tt(t):
            t.wait_ge(k.sems["ro"], max(0, ro0 + NTA))
            t.wait_ge(k.sems["ev"], max(0, ev0 + 3 * (NTA - 2)) + 3)
            for b4i in range(4):
                inst = t.transpose(Bx.pb[2 + ((NTA - 1) % 2)][:, b4i * 128:(b4i + 1) * 128],
                                   Bx.ktmp[:, b4i * 128:(b4i + 1) * 128], Bx.ident[:])
            k.inc(inst, "tr", 1)
        block.tensor(tt)

        def st(s):
            k.w(s, "tr")
            k.inc(s.copy(out=kf4[:, :, (NTA - 1) * 128:NTA * 128],
                         in_=Bx.pb[2 + ((NTA - 1) % 2)][:, 0:512]
                         .rearrange("p (g c) -> p g c", g=4)), "ev", 1)
        block.scalar(st)


def _p3_qproj(k, block, Bx, Q):
    def syw(sy):
        if Q == 0:
            k.w(sy, "ev")
        k.inc(sy.dma_start(out=Bx.wqo[:].rearrange("p (c n) -> p c n", c=NC),
                           in_=Bx.wq_d[:, Q * 512:(Q + 1) * 512]
                           .rearrange("(c p) n -> p c n", p=128)), "w2", 16)
    block.sync(syw)
    n_wq = k.cnt["w2"]

    ln0 = k.cnt["ln"]
    mm0 = k.cnt["mm"]
    ev0 = k.cnt["ev"]
    ro0 = k.cnt["ro"]
    tr0 = k.cnt["tr"]
    n_acc_q = k.cnt["acc"]
    qf4 = Bx.q_fm[:].rearrange("p (hh t) -> p hh t", hh=4)

    for i in range(NTO):
        par = i % 2
        pn = "xe" if par == 0 else "xo"

        def syd(sy, i=i, par=par, pn=pn):
            if i >= 2:
                k.w(sy, "ln", ln0 + i - 1)
            k.xld[par] += 1
            k.inc(sy.dma_start(out=Bx.xin[par][:].rearrange("p (c j) -> p c j", c=NC),
                               in_=Bx.x_fm4[i].rearrange("c p j -> p c j")), pn, 16)
        block.sync(syd)

        def ve(v, i=i, par=par, pn=pn):
            v.wait_ge(k.sems[pn], 16 * k.xld[par])
            if i >= 2:
                v.wait_ge(k.sems["mm"], max(0, mm0 + i - 1))
            _ln_apply_fm(k, v, Bx, i, par)
            k.inc(v.engine_nop(), "ln", 1)
        block.vector(ve)

        def te(t, i=i, par=par):
            if i == 0:
                k.w(t, "w2", n_wq)
            k.w(t, "ln", ln0 + i + 1)
            if i >= 1:
                t.wait_ge(k.sems["ev"], max(0, ev0 + 2 * (i - 1)) + 1)
            xt = Bx.xnb[par]
            for c in range(NC):
                inst = t.matmul(Bx.pb[0][:, 0:512], xt[:, c * 128:(c + 1) * 128],
                                Bx.wqo[:, c * 512:(c + 1) * 512],
                                start=(c == 0), stop=(c == NC - 1))
            k.inc(inst, "mm", 1)
            if i >= 1:
                t.wait_ge(k.sems["ro"], max(0, ro0 + i))
                if i >= 3:
                    t.wait_ge(k.sems["ev"], max(0, ev0 + 2 * (i - 3)) + 4)
                for b4i in range(4):
                    inst = t.transpose(Bx.pb[2 + ((i - 1) % 2)][:, b4i * 128:(b4i + 1) * 128],
                                       Bx.ktmp[:, b4i * 128:(b4i + 1) * 128], Bx.ident[:])
                k.inc(inst, "tr", 1)
        block.tensor(te)

        def se(s, i=i):
            s.wait_ge(k.sems["mm"], max(0, mm0 + i + 1))
            if i == 0:
                s.wait_ge(k.sems["acc"], max(0, n_acc_q))
            if i >= 1:
                s.wait_ge(k.sems["tr"], max(0, tr0 + i))
            k.inc(s.copy(out=Bx.ktmp[:], in_=Bx.pb[0][:, 0:512]), "ev", 1)
            if i >= 1:
                k.inc(s.copy(out=qf4[:, :, (i - 1) * 128:i * 128],
                             in_=Bx.pb[2 + ((i - 1) % 2)][:, 0:512]
                             .rearrange("p (hh c) -> p hh c", hh=4)), "ev", 1)
            else:
                k.inc(s.copy(out=Bx.sc11[0:1, 15:16], in_=Bx.ones1[0:1, 0:1]),
                      "ev", 1)
        block.scalar(se)

        def vr(v, i=i):
            v.wait_ge(k.sems["ev"], max(0, ev0 + 2 * i + 1))
            _rope(k, v, Bx, i, Bx.ktmp)
            k.inc(v.engine_nop(), "ro", 1)
        block.vector(vr)

    def tt(t):
        t.wait_ge(k.sems["ro"], max(0, ro0 + NTO))
        t.wait_ge(k.sems["ev"], max(0, ev0 + 2 * (NTO - 3)) + 4)
        for b4i in range(4):
            inst = t.transpose(Bx.pb[2 + ((NTO - 1) % 2)][:, b4i * 128:(b4i + 1) * 128],
                               Bx.ktmp[:, b4i * 128:(b4i + 1) * 128], Bx.ident[:])
        k.inc(inst, "tr", 1)
    block.tensor(tt)

    def st(s):
        k.w(s, "tr")
        k.inc(s.copy(out=qf4[:, :, (NTO - 1) * 128:NTO * 128],
                     in_=Bx.pb[2 + ((NTO - 1) % 2)][:, 0:512]
                     .rearrange("p (hh c) -> p hh c", hh=4)), "ev", 1)
    block.scalar(st)


def _p3_attn(k, block, Bx, Q, nhh=4, nqs=2):
    E0 = k.cnt["exp"]
    A0 = k.cnt["av"]
    V0 = k.cnt["avv"]
    TV0 = k.cnt["trv"]
    n_ev_q = k.cnt["ev"]
    kf4 = Bx.k_fm[:].rearrange("p (g ta) -> p g ta", g=4)
    vv4 = Bx.v_sb[:].rearrange("p (t g c) -> p t g c", t=NTA, g=4)
    of4 = Bx.oh_fm[:].rearrange("p (hh t) -> p hh t", hh=4)

    for hh in range(nhh):
        for qs in range(nqs):
            it = hh * 2 + qs

            def te(t, hh=hh, qs=qs, it=it):
                if it == 0:
                    t.wait_ge(k.sems["ev"], max(0, n_ev_q))
                t.wait_ge(k.sems["avv"], max(0, V0 + it))
                t.wait_ge(k.sems["trv"], max(0, TV0 + it))
                for kt in range(NTA):
                    t.wait_ge(k.sems["exp"], max(0, E0 + it * 16 + kt - 1))
                    k.inc(t.matmul(Bx.pb[kt % 2][:, 0:512],
                                   kf4[:, Q, kt * 128:(kt + 1) * 128],
                                   Bx.q_fm[:, hh * TO + qs * 512: hh * TO + (qs + 1) * 512],
                                   start=True, stop=True), "mm", 1)
                    if kt >= 2:
                        kt2 = kt - 2
                        for sub in range(4):
                            inst = t.matmul(Bx.pb[2 + sub][:, 0:129],
                                            Bx.ptile[kt2 % 2][:, sub * 128:(sub + 1) * 128],
                                            vv4[:, kt2, Q, 0:129],
                                            start=(kt2 == 0), stop=False)
                        k.inc(inst, "av", 1)
                for kt2 in (NTA - 2, NTA - 1):
                    t.wait_ge(k.sems["exp"], max(0, E0 + it * 16 + kt2 + 1))
                    for sub in range(4):
                        inst = t.matmul(Bx.pb[2 + sub][:, 0:129],
                                        Bx.ptile[kt2 % 2][:, sub * 128:(sub + 1) * 128],
                                        vv4[:, kt2, Q, 0:129],
                                        start=False, stop=(kt2 == NTA - 1))
                    k.inc(inst, "av", 1)
            block.tensor(te)
            mm_after_scores = k.cnt["mm"]

            def se(s, it=it):
                for kt in range(NTA):
                    s.wait_ge(k.sems["mm"], max(0, mm_after_scores - (NTA - 1 - kt)))
                    s.wait_ge(k.sems["av"], max(0, A0 + it * 16 + kt - 1))
                    k.inc(s.activation(out=Bx.ptile[kt % 2][:],
                                       in_=Bx.pb[kt % 2][:, 0:512],
                                       func=AF.Exp, scale=SCALE_ATT), "exp", 1)
            block.scalar(se)

            def va(v, it=it):
                v.wait_ge(k.sems["av"], max(0, A0 + (it + 1) * 16))
                for sub in range(4):
                    v.reciprocal(out=Bx.rcp4[:, sub:sub + 1],
                                 in_=Bx.pb[2 + sub][:, 128:129])
                for sub in range(4):
                    v.tensor_tensor(out=Bx.rn4[:, sub:sub + 1],
                                    in0=Bx.pb[2 + sub][:, 128:129],
                                    in1=Bx.rcp4[:, sub:sub + 1], op=OP.mult)
                for sub in range(4):
                    v.tensor_scalar(out=Bx.rn4[:, sub:sub + 1],
                                    in0=Bx.rn4[:, sub:sub + 1],
                                    scalar1=-1.0, scalar2=2.0,
                                    op0=OP.mult, op1=OP.add)
                for sub in range(4):
                    v.tensor_tensor(out=Bx.rcp4[:, sub:sub + 1],
                                    in0=Bx.rcp4[:, sub:sub + 1],
                                    in1=Bx.rn4[:, sub:sub + 1], op=OP.mult)
                for sub in range(4):
                    v.tensor_scalar(out=Bx.ohtmp[:, sub * 128:(sub + 1) * 128],
                                    in0=Bx.pb[2 + sub][:, 0:128],
                                    scalar1=Bx.rcp4[:, sub:sub + 1], scalar2=None,
                                    op0=OP.mult)
                k.inc(v.engine_nop(), "avv", 1)
            block.vector(va)

            def tt(t, it=it):
                t.wait_ge(k.sems["avv"], max(0, V0 + it + 1))
                for sub in range(4):
                    inst = t.transpose(Bx.pb[0][:, sub * 128:(sub + 1) * 128],
                                       Bx.ohtmp[:, sub * 128:(sub + 1) * 128], Bx.ident[:])
                k.inc(inst, "tr", 1)
            block.tensor(tt)

            def so(s, it=it, hh=hh, qs=qs):
                k.w(s, "tr")
                k.inc(s.copy(out=of4[:, hh, qs * 512:(qs + 1) * 512],
                             in_=Bx.pb[0][:, 0:512]), "trv", 1)
            block.scalar(so)


def _p3_wo(k, block, Bx, Q):
    stage = [Bx.ktmp, Bx.ohtmp]
    TV_end = k.cnt["trv"]
    mm0 = k.cnt["mm"]
    st0 = k.cnt["st"]
    acc0 = k.cnt["acc"]
    of4 = Bx.oh_fm[:].rearrange("p (hh t) -> p hh t", hh=4)
    ev_marks = {}

    for os in range(8):
        def syw(sy, os=os):
            if os == 0:
                k.w(sy, "trv", TV_end)
            else:
                k.w(sy, "mm", mm0 + os * 8)
            k.inc(sy.dma_start(out=Bx.wos[:].rearrange("p (c n) -> p c n", c=4),
                               in_=Bx.wo_d[Q * 512:(Q + 1) * 512,
                                           os * 256:(os + 1) * 256]
                               .rearrange("(c p) n -> p c n", p=128)), "w2", 16)
        block.sync(syw)
        n_wo = k.cnt["w2"]

        for qt in range(NTO):
            idx = os * 8 + qt
            par = qt % 2

            if Q == 0:
                def syx(sy, os=os, qt=qt, par=par, idx=idx):
                    pn = "xe" if par == 0 else "xo"
                    if idx >= 2:
                        sy.wait_ge(k.sems["ev"], max(0, ev_marks[idx - 2]))
                    k.xld[par] += 1
                    k.inc(sy.dma_start(out=Bx.xin[par][:, 0:256],
                                       in_=Bx.x_tm[qt * 128:(qt + 1) * 128,
                                                   os * 256:(os + 1) * 256]), pn, 16)
                block.sync(syx)

            def te(t, os=os, qt=qt, par=par, idx=idx, n_wo=n_wo):
                if qt == 0:
                    k.w(t, "w2", n_wo)
                    if os == 0:
                        k.w(t, "trv", TV_end)
                if idx >= 2:
                    t.wait_ge(k.sems["ev"], max(0, ev_marks[idx - 2]))
                for c in range(4):
                    inst = t.matmul(Bx.pb[par][:, 0:256],
                                    of4[:, c, qt * 128:(qt + 1) * 128],
                                    Bx.wos[:, c * 256:(c + 1) * 256],
                                    start=(c == 0), stop=(c == 3))
                k.inc(inst, "mm", 1)
            block.tensor(te)
            mm_here = k.cnt["mm"]

            if Q == 0:
                def vev(v, os=os, qt=qt, par=par, idx=idx, mm_here=mm_here):
                    pn = "xe" if par == 0 else "xo"
                    v.wait_ge(k.sems[pn], 16 * k.xld[par])
                    v.wait_ge(k.sems["mm"], max(0, mm_here))
                    if idx >= 2:
                        v.wait_ge(k.sems["st"], max(0, st0 + 16 * (idx - 1)))
                    k.inc(v.tensor_tensor(out=stage[par][:, 0:256],
                                          in0=Bx.pb[par][:, 0:256],
                                          in1=Bx.xin[par][:, 0:256], op=OP.add),
                          "ev", 1)
                block.vector(vev)
                ev_marks[idx] = k.cnt["ev"]

                def sys(sy, os=os, qt=qt, par=par):
                    k.w(sy, "ev")
                    k.inc(sy.dma_start(out=Bx.out_d[qt * 128:(qt + 1) * 128,
                                                    os * 256:(os + 1) * 256],
                                       in_=stage[par][:, 0:256]), "st", 16)
                block.sync(sys)
            else:
                def sev(s, os=os, qt=qt, par=par, idx=idx, mm_here=mm_here):
                    s.wait_ge(k.sems["mm"], max(0, mm_here))
                    if idx >= 2:
                        s.wait_ge(k.sems["acc"], max(0, acc0 + 16 * (idx - 1)))
                    else:
                        s.wait_ge(k.sems["acc"], max(0, acc0))
                        s.wait_ge(k.sems["st"], max(0, st0))
                    k.inc(s.copy(out=stage[par][:, 0:256],
                                 in_=Bx.pb[par][:, 0:256]), "ev", 1)
                block.scalar(sev)
                ev_marks[idx] = k.cnt["ev"]

                def gac(g, os=os, qt=qt, par=par, idx=idx):
                    k.w(g, "ev")
                    if Q == 1 and idx == 0:
                        k.w(g, "st")
                    if NOACC:
                        k.inc(g.dma_start(out=Bx.h_hbm[qt * 128:(qt + 1) * 128,
                                                       os * 256:(os + 1) * 256],
                                          in_=stage[par][:, 0:256]), "acc", 16)
                    else:
                        k.inc(g.dma_start(out=Bx.out_d[qt * 128:(qt + 1) * 128,
                                                       os * 256:(os + 1) * 256],
                                          in_=stage[par][:, 0:256], accum_op=OP.add),
                              "acc", 16)
                block.gpsimd(gac)

    def syq(sy):
        k.w(sy, "ev")
        if Q == 0:
            k.w(sy, "st")
        else:
            k.w(sy, "acc")
    block.sync(syq)


def _p5a(k, block, Bx):
    nc = k.nc
    with contextlib.ExitStack() as esa:
        wmn = [esa.enter_context(nc.sbuf_tensor(f"wmn{i}", [128, 4096], F32))
               for i in range(2)]
        wpart = esa.enter_context(nc.sbuf_tensor("wpart", [128, 128], F32))

        pieces = []
        for c in range(NC):
            pieces.append((Bx.w1t_d[c * 128:(c + 1) * 128, 0:4096], 2 * c, 4096))
            pieces.append((Bx.w1t_d[c * 128:(c + 1) * 128, 4096:8192], 2 * c + 1, 4096))
        for c in range(NC):
            pieces.append((Bx.w2t_d[c * 128:(c + 1) * 128, 0:4096], 32 + 2 * c, 4096))
            pieces.append((Bx.w2t_d[c * 128:(c + 1) * 128, 4096:8192], 32 + 2 * c + 1, 4096))
        for c in range(64):
            pieces.append((Bx.w3t_d[c * 128:(c + 1) * 128, :], 64 + c, 2048))

        wm0 = k.cnt["wm"]
        for n_i, (src, col, wd) in enumerate(pieces):
            par = n_i % 2

            def syd(sy, n_i=n_i, src=src, par=par, wd=wd):
                if n_i >= 2:
                    k.w(sy, "wm", wm0 + n_i - 1)
                k.inc(sy.dma_start(out=wmn[par][:, 0:wd], in_=src), "w3", 16)
            block.sync(syd)
            n_w3 = k.cnt["w3"]

            def ve(v, n_i=n_i, col=col, par=par, n_w3=n_w3, wd=wd):
                v.wait_ge(k.sems["w3"], max(0, n_w3))
                k.inc(v.tensor_reduce(out=wpart[:, col:col + 1],
                                      in_=wmn[par][:, 0:wd],
                                      axis=AX, op=OP.add, apply_absolute_value=True),
                      "wm", 1)
            block.vector(ve)
        n_wm = k.cnt["wm"]

        def vred(v):
            k.w(v, "wm", n_wm)
            v.tensor_reduce(out=Bx.stsc[:, 20:21], in_=wpart[:, 0:32], axis=AX, op=OP.add)
            v.tensor_reduce(out=Bx.stsc[:, 21:22], in_=wpart[:, 32:64], axis=AX, op=OP.add)
            v.tensor_reduce(out=Bx.stsc[:, 22:23], in_=wpart[:, 64:128], axis=AX, op=OP.add)
            k.inc(v.engine_nop(), "b1", 1)
        block.vector(vred)

        def tred(t):
            k.w(t, "b1")
            k.inc(t.matmul(Bx.pb[0][0:3, 0:1], Bx.stsc[:, 20:23], Bx.onesc[:],
                           start=True, stop=True), "mm", 1)
        block.tensor(tred)

        def sev2(s):
            k.w(s, "mm")
            k.inc(s.copy(out=Bx.flip48[0:3, 0:1], in_=Bx.pb[0][0:3, 0:1]), "ev", 1)
        block.scalar(sev2)

        def syb(sy):
            k.w(sy, "ev")
            k.inc(sy.dma_start(out=Bx.fl_hbm[0:3, 0:1], in_=Bx.flip48[0:3, 0:1],
                               allow_slow_non_contiguous=True), "st", 16)
            k.w(sy, "st")
            fl = Bx.fl_hbm.rearrange("a b -> (a b)")
            k.inc(sy.dma_start(out=Bx.sc11[0:1, 4:7],
                               in_=fl[0:3 * 128:128].rearrange("(o x) -> o x", o=1),
                               allow_slow_non_contiguous=True), "w", 16)
        block.sync(syb)
        n_ws = k.cnt["w"]

        def vth(v):
            k.w(v, "w", n_ws)
            for i in range(3):
                v.tensor_scalar(out=Bx.sc11[0:1, 4 + i:5 + i],
                                in0=Bx.sc11[0:1, 4 + i:5 + i],
                                scalar1=1.0 / (D * HF), scalar2=None, op0=OP.mult)
            k.inc(v.tensor_scalar(out=Bx.sc11[0:1, 2:3], in0=Bx.ones1[0:1, 0:1],
                                  scalar1=0.7320508075688772, scalar2=None,
                                  op0=OP.add), "b3", 1)
        block.vector(vth)

        def sln(s):
            k.w(s, "b3")
            k.inc(s.activation(out=Bx.sc11[0:1, 3:4], in_=Bx.sc11[0:1, 2:3],
                               func=AF.Ln), "b3", 1)
        block.scalar(sln)

        def vth2(v):
            k.w(v, "b3")
            v.tensor_scalar(out=Bx.a_sc[0:1, 0:3], in0=Bx.sc11[0:1, 4:7],
                            scalar1=Bx.sc11[0:1, 3:4],
                            scalar2=float(EPS_T * 0.5493061443340548),
                            op0=OP.mult, op1=OP.add)
            v.tensor_scalar(out=Bx.a_sc[0:1, 6:9], in0=Bx.sc11[0:1, 4:7],
                            scalar1=1.0, scalar2=None, op0=OP.add)
            k.inc(v.tensor_scalar(out=Bx.a_sc[0:1, 9:12], in0=Bx.sc11[0:1, 4:7],
                                  scalar1=-1.0, scalar2=1.0,
                                  op0=OP.mult, op1=OP.add), "b3", 1)
        block.vector(vth2)

        def sln2(s):
            k.w(s, "b3")
            s.activation(out=Bx.a_sc[0:1, 6:9], in_=Bx.a_sc[0:1, 6:9], func=AF.Ln)
            k.inc(s.activation(out=Bx.a_sc[0:1, 9:12], in_=Bx.a_sc[0:1, 9:12],
                               func=AF.Ln), "b3", 1)
        block.scalar(sln2)

        def vhalf(v):
            k.w(v, "b3")
            k.inc(v.scalar_tensor_tensor(out=Bx.a_sc[0:1, 3:6],
                                         in0=Bx.a_sc[0:1, 9:12], scalar=-1.0,
                                         in1=Bx.a_sc[0:1, 6:9],
                                         op0=OP.mult, op1=OP.add), "b3", 1)
        block.vector(vhalf)

        def tb(t):
            k.w(t, "b3")
            for i in range(3):
                k.inc(t.matmul(Bx.pb[0][:, i:i + 1], Bx.ones1[:],
                               Bx.a_sc[0:1, i:i + 1], start=True, stop=True), "mm", 1)
            for i in range(3):
                k.inc(t.matmul(Bx.pb[1][:, i:i + 1], Bx.ones1[:],
                               Bx.a_sc[0:1, 3 + i:4 + i], start=True, stop=True),
                      "mm", 1)
        block.tensor(tb)

        def stb(s):
            k.w(s, "mm")
            for i in range(3):
                k.inc(s.copy(out=Bx.t_bc[i][:], in_=Bx.pb[0][:, i:i + 1]), "ev", 1)
            for i in range(3):
                k.inc(s.copy(out=Bx.nt_bc[i][:], in_=Bx.pb[1][:, i:i + 1]), "ev", 1)
        block.scalar(stb)

        def vsig(v):
            k.w(v, "ev")
            for i, (iscx, sigx) in enumerate(((Bx.isc1, Bx.sig1), (Bx.isc2, Bx.sig2))):
                v.tensor_scalar(out=sigx[:], in0=iscx[:], scalar1=Bx.nt_bc[i][:],
                                scalar2=-0.5, op0=OP.mult, op1=OP.mult)
            v.tensor_copy(out=Bx.a3_bc[:], in_=Bx.nt_bc[2][:])
            for i in range(3):
                v.tensor_scalar(out=Bx.nt_bc[i][:], in0=Bx.t_bc[i][:], scalar1=-1.0,
                                scalar2=None, op0=OP.mult)
            k.inc(v.engine_nop(), "b4", 1)
        block.vector(vsig)


def _p5b(k, block, Bx):
    nc = k.nc
    with contextlib.ExitStack() as esb:
        gam1_bc = esb.enter_context(nc.sbuf_tensor("gam1_bc", [128, TO], F32))
        gam2_bc = esb.enter_context(nc.sbuf_tensor("gam2_bc", [128, TO], F32))

        fl = Bx.fl_hbm.rearrange("a b -> (a b)")

        def syg1(sy):
            k.w(sy, "mm")
            k.inc(sy.dma_start(out=Bx.row_s[:, 0:TO],
                               in_=fl[4096:5120].rearrange("(o x) -> o x", o=1)),
                  "w", 16)
        block.sync(syg1)
        n_wgm1 = k.cnt["w"]

        def tw(t):
            k.w(t, "b4")
            k.w(t, "w", n_wgm1)
        block.tensor(tw)
        _bcast_row(k, block, Bx, Bx.row_s, gam1_bc, TO)
        n_mm_b1 = k.cnt["mm"]

        def syg2(sy):
            k.w(sy, "mm", n_mm_b1)
            k.inc(sy.dma_start(out=Bx.row_s[:, 0:TO],
                               in_=fl[5120:6144].rearrange("(o x) -> o x", o=1)),
                  "w", 16)
        block.sync(syg2)
        n_wgm2 = k.cnt["w"]

        def tw2(t):
            k.w(t, "w", n_wgm2)
        block.tensor(tw2)
        _bcast_row(k, block, Bx, Bx.row_s, gam2_bc, TO)
        n_evb = k.cnt["ev"]

        pe0 = k.cnt["pe"]
        lad0 = k.cnt["lad"]
        ln0 = k.cnt["ln"]
        n_inst = 0
        for c in range(NC):
            par = c % 2
            pn = "xe" if par == 0 else "xo"

            def syd(sy, c=c, par=par, pn=pn):
                if c >= 2:
                    k.w(sy, "ln", ln0 + c - 1)
                k.xld[par] += 1
                k.inc(sy.dma_start(out=Bx.xin[par][:, 0:1024]
                                   .rearrange("p (t j) -> p t j", t=NTO),
                                   in_=Bx.x_fm4[0:NTO, c].rearrange("t p j -> p t j")), pn, 16)
            block.sync(syd)

            def vln(v, c=c, par=par, pn=pn):
                if c == 0:
                    v.wait_ge(k.sems["ev"], max(0, n_evb))
                v.wait_ge(k.sems[pn], 16 * k.xld[par])
                xs = Bx.xin[par][:, 0:1024]
                xo = Bx.xnb[par][:, 0:1024]
                v.tensor_tensor(out=xo, in0=xs, in1=Bx.m_bc[:, 0:1024], op=OP.subtract)
                v.tensor_tensor(out=xo, in0=xo, in1=Bx.r_bc[:, 0:1024], op=OP.mult)
                k.inc(v.engine_nop(), "ln", 1)
            block.vector(vln)

            for gf, gbc, kdst in ((Bx.g1f, gam1_bc, Bx.k1), (Bx.g2f, gam2_bc, Bx.k2)):
                lp = n_inst % 2

                def yem(v, c=c, par=par, gf=gf, gbc=gbc, lp=lp):
                    k.w(v, "ln", ln0 + c + 1)
                    v.scalar_tensor_tensor(out=Bx.lz[lp][:, 0:1024],
                                           in0=Bx.xnb[par][:, 0:1024],
                                           scalar=gf[:, c:c + 1], in1=gbc[:],
                                           op0=OP.mult, op1=OP.mult)

                def dem(v, c=c, kdst=kdst, lp=lp):
                    v.tensor_scalar(out=kdst[:, c * 1024:(c + 1) * 1024],
                                    in0=Bx.lb[lp][:, 0:1024], scalar1=127.0,
                                    scalar2=None, op0=OP.min)

                _ladder_emit(k, block, Bx, n_inst, lp, 1024, yem, dem, pe0, lad0)
                n_inst += 1


def _p5c(k, block, Bx):
    nc = k.nc
    with contextlib.ExitStack() as esc:
        wbuf = esc.enter_context(nc.sbuf_tensor("wbuf", [128, 8 * 512], F32))
        n1 = esc.enter_context(nc.sbuf_tensor("n1_sb", [128, NC * 512], BF16))
        n2 = esc.enter_context(nc.sbuf_tensor("n2_sb", [128, NC * 512], BF16))
        g3bc = esc.enter_context(nc.sbuf_tensor("g3bc", [128, 512], F32))
        hst = [esc.enter_context(nc.sbuf_tensor(f"hst{i}", [128, 512], F32))
               for i in range(2)]
        su_t = esc.enter_context(nc.sbuf_tensor("su_t", [128, 512], F32))
        sq_t = esc.enter_context(nc.sbuf_tensor("sq_t", [128, 512], F32))

        tern0 = k.cnt["ro"]
        h0 = k.cnt["h"]
        ss_marks = {}
        vh_marks = {}
        sg_marks = {}
        sg_mmm = {}
        umm_marks = {}
        pe_all = k.cnt["pe"]

        for os in range(16):
            for half, (wt_d, nn, tix) in enumerate(
                    [(Bx.w1t_d, n1, 0), (None, None, None),
                     (Bx.w2t_d, n2, 1), (None, None, None)]):
                if wt_d is None:
                    continue
                for hp in range(2):
                    t_i = 4 * os + 2 * half // 2 + 0
                    def sywx(sy, os=os, wt_d=wt_d, hp=hp, half=half):
                        if os == 0 and half == 0 and hp == 0:
                            k.w(sy, "pe", pe_all)
                        k.w(sy, "ro", k.cnt["ro"])
                        if os >= 1 and half == 0 and hp == 0:
                            pass
                        k.inc(sy.dma_start(
                            out=wbuf[:].rearrange("p (c n) -> p c n", c=8),
                            in_=wt_d[hp * 1024:(hp + 1) * 1024,
                                     os * 512:(os + 1) * 512]
                            .rearrange("(c p) n -> p c n", p=128)), "w3", 16)
                    block.sync(sywx)
                    nwx = k.cnt["w3"]

                    def vtx(v, os=os, hp=hp, nn=nn, tix=tix, nwx=nwx):
                        v.wait_ge(k.sems["w3"], max(0, nwx))
                        if os >= 1 and hp == 0 and tix == 0:
                            v.wait_ge(k.sems["mm"], max(0, umm_marks[(os - 1, NTO - 1)]))
                        dst = nn[:, hp * 8 * 512:(hp + 1) * 8 * 512]
                        v.tensor_scalar(out=dst, in0=wbuf[:], scalar1=Bx.t_bc[tix][:],
                                        scalar2=None, op0=OP.is_gt)
                        k.inc(v.scalar_tensor_tensor(out=dst, in0=wbuf[:],
                                                     scalar=Bx.nt_bc[tix][:], in1=dst,
                                                     op0=OP.is_lt, op1=OP.subtract),
                              "ro", 1)
                    block.vector(vtx)

            def syg3(sy, os=os):
                if os >= 1:
                    k.w(sy, "mm", sg_mmm[os - 1])
                k.inc(sy.dma_start(out=Bx.g3s[:],
                                   in_=Bx.g3r_d[0:1, os * 512:(os + 1) * 512]),
                      "c", 16)
            block.sync(syg3)
            nc3 = k.cnt["c"]

            def tg(t, os=os, nc3=nc3):
                t.wait_ge(k.sems["c"], max(0, nc3))
                if os >= 1:
                    t.wait_ge(k.sems["ev"], max(0, sg_marks[os - 1] + 0))
                    t.wait_ge(k.sems["av"], max(0, k.cnt["av"]))
                k.inc(t.matmul(Bx.pb[4][:, 0:512], Bx.ones1[:], Bx.g3s[:],
                               start=True, stop=True), "mm", 1)
            block.tensor(tg)
            mm_g = k.cnt["mm"]
            sg_mmm[os] = mm_g

            def sg(s, os=os, mm_g=mm_g):
                s.wait_ge(k.sems["mm"], max(0, mm_g))
                k.inc(s.copy(out=g3bc[:], in_=Bx.pb[4][:, 0:512]), "ev", 1)
            block.scalar(sg)
            sg_marks[os] = k.cnt["ev"]

            for tt in range(NTO):
                par = tt % 2
                idx = os * 8 + tt

                def tu(t, os=os, tt=tt, par=par, idx=idx):
                    if tt == 0:
                        k.w(t, "ro", tern0 + 4 * os + 4)
                    if idx >= 2:
                        t.wait_ge(k.sems["ev"], max(0, ss_marks[idx - 2]))
                        t.wait_ge(k.sems["ev"], max(0, vh_marks[idx - 2]))
                    for c in range(NC):
                        inst = t.matmul(Bx.pb[par][:, 0:512],
                                        Bx.k1[:, c * TO + tt * 128: c * TO + (tt + 1) * 128],
                                        n1[:, c * 512:(c + 1) * 512],
                                        start=(c == 0), stop=(c == NC - 1))
                    k.inc(inst, "mm", 1)
                    for c in range(NC):
                        inst = t.matmul(Bx.pb[2 + par][:, 0:512],
                                        Bx.k2[:, c * TO + tt * 128: c * TO + (tt + 1) * 128],
                                        n2[:, c * 512:(c + 1) * 512],
                                        start=(c == 0), stop=(c == NC - 1))
                    k.inc(inst, "mm", 1)
                block.tensor(tu)
                umm_marks[(os, tt)] = k.cnt["mm"]
                mm_u = k.cnt["mm"]

                def ss(s, os=os, tt=tt, par=par, mm_u=mm_u):
                    s.wait_ge(k.sems["mm"], max(0, mm_u - 1))
                    k.inc(s.activation(out=su_t[:], in_=Bx.pb[par][:, 0:512],
                                       func=AF.Silu, scale=Bx.sig1[:, tt:tt + 1]),
                          "ev", 1)
                block.scalar(ss)
                ss_marks[idx] = k.cnt["ev"]
                ev_su = k.cnt["ev"]

                def vh(v, os=os, tt=tt, par=par, mm_u=mm_u, ev_su=ev_su, idx=idx):
                    v.wait_ge(k.sems["mm"], max(0, mm_u))
                    v.wait_ge(k.sems["ev"], max(0, ev_su))
                    if idx >= 2:
                        v.wait_ge(k.sems["h"], max(0, h0 + 16 * (idx - 1)))
                    k.inc(v.scalar_tensor_tensor(out=hst[par][:],
                                                 in0=Bx.pb[2 + par][:, 0:512],
                                                 scalar=Bx.sig2[:, tt:tt + 1],
                                                 in1=su_t[:], op0=OP.mult, op1=OP.mult),
                          "ev", 1)
                block.vector(vh)
                vh_marks[idx] = k.cnt["ev"]
                ev_h = k.cnt["ev"]

                def syh(sy, os=os, tt=tt, par=par, ev_h=ev_h):
                    k.w(sy, "ev", ev_h)
                    k.inc(sy.dma_start(out=Bx.h_hbm[tt * 128:(tt + 1) * 128,
                                                    os * 512:(os + 1) * 512],
                                       in_=hst[par][:]), "h", 16)
                block.sync(syh)

                def sqe(s, os=os, tt=tt, par=par, ev_h=ev_h):
                    k.w(s, "ev", ev_h)
                    k.inc(s.activation(out=sq_t[:], in_=hst[par][:], func=AF.Square,
                                       accum_out=Bx.ssq3[:, tt * 16 + os:
                                                         tt * 16 + os + 1]), "ev", 1)
                block.scalar(sqe)

                def vm(v, os=os, tt=tt, par=par):
                    k.w(v, "ev")
                    v.tensor_tensor(out=sq_t[:], in0=hst[par][:], in1=g3bc[:],
                                    op=OP.mult)
                    k.inc(v.reduce_max(out=Bx.rm3c[:, tt * 16 + os: tt * 16 + os + 1],
                                       in_=sq_t[:], axis=AX,
                                       apply_absolute_value=True), "av", 1)
                block.vector(vm)


def _p5d(k, block, Bx):
    def vt(v):
        k.w(v, "av")
        k.w(v, "ev")
        ms = Bx.stsc[:, 0:8]
        for tt in range(NTO):
            v.tensor_reduce(out=ms[:, tt:tt + 1],
                            in_=Bx.ssq3[:, tt * 16:(tt + 1) * 16], axis=AX, op=OP.add)
            v.tensor_reduce(out=Bx.stsc[:, 8 + tt:9 + tt],
                            in_=Bx.rm3c[:, tt * 16:(tt + 1) * 16], axis=AX, op=OP.max)
        GG(v)
        v.tensor_scalar(out=ms[:], in0=ms[:], scalar1=1.0 / HF, scalar2=EPS_RMS,
                        op0=OP.mult, op1=OP.add)
        k.inc(v.engine_nop(), "b4", 1)
    block.vector(vt)

    def st(s):
        k.w(s, "b4")
        k.inc(s.activation(out=Bx.stsc[:, 16:24], in_=Bx.stsc[:, 0:8], func=AF.Sqrt),
              "b4", 1)
    block.scalar(st)

    def vt2(v):
        k.w(v, "b4")
        ms = Bx.stsc[:, 0:8]
        rmx = Bx.stsc[:, 8:16]
        rr = Bx.gam3
        t1 = Bx.sig3
        r2 = Bx.stsc[:, 16:24]
        v.reciprocal(out=rr[:], in_=r2[:])
        GG(v)
        v.tensor_mul(out=t1[:], in0=rr[:], in1=rr[:])
        GG(v)
        v.tensor_mul(out=t1[:], in0=t1[:], in1=ms[:])
        GG(v)
        v.tensor_scalar(out=t1[:], in0=t1[:], scalar1=-0.5, scalar2=1.5,
                        op0=OP.mult, op1=OP.add)
        GG(v)
        v.tensor_mul(out=rr[:], in0=rr[:], in1=t1[:])
        GG(v)
        v.tensor_mul(out=t1[:], in0=rr[:], in1=rmx[:])
        GG(v)
        v.tensor_scalar(out=t1[:], in0=t1[:], scalar1=1e-4, scalar2=None, op0=OP.max)
        GG(v)
        v.reciprocal(out=r2[:], in_=t1[:])
        GG(v)
        v.tensor_mul(out=ms[:], in0=r2[:], in1=t1[:])
        GG(v)
        v.tensor_scalar(out=ms[:], in0=ms[:], scalar1=-1.0, scalar2=2.0,
                        op0=OP.mult, op1=OP.add)
        GG(v)
        v.tensor_mul(out=r2[:], in0=r2[:], in1=ms[:])
        GG(v)
        v.tensor_mul(out=r2[:], in0=r2[:], in1=rr[:])
        GG(v)
        v.tensor_scalar(out=Bx.gam3[:], in0=r2[:], scalar1=127.0, scalar2=None,
                        op0=OP.mult)
        v.tensor_scalar(out=t1[:], in0=t1[:], scalar1=1.0 / 127.0, scalar2=None,
                        op0=OP.mult)
        GG(v)
        k.inc(v.tensor_scalar(out=Bx.sig3[:], in0=t1[:], scalar1=Bx.a3_bc[:],
                              scalar2=-0.5, op0=OP.mult, op1=OP.mult), "b4", 1)
    block.vector(vt2)


def _p5e(k, block, Bx):
    n_h_all = k.cnt["h"]
    pe0 = k.cnt["pe"]
    lad0 = k.cnt["lad"]
    kf3 = Bx.k3[:].rearrange("p (o t) -> p o t", o=64)
    g_marks = {}
    n_i = 0

    g_mmm = {}
    for os in range(16):
        def syg3(sy, os=os):
            if os >= 1:
                k.w(sy, "mm", g_mmm[os - 1])
            k.inc(sy.dma_start(out=Bx.g3s[:],
                               in_=Bx.g3r_d[0:1, os * 512:(os + 1) * 512]), "c", 16)
        block.sync(syg3)
        nc3 = k.cnt["c"]

        def tg(t, os=os, nc3=nc3):
            t.wait_ge(k.sems["c"], max(0, nc3))
            if os == 0:
                k.w(t, "b4")
                k.w(t, "mm")
            else:
                t.wait_ge(k.sems["pe"], max(0, pe0 + (n_i - 1) * 10 + 1))
            k.inc(t.matmul(Bx.pb[4][:, 0:512], Bx.ones1[:], Bx.g3s[:],
                           start=True, stop=True), "mm", 1)
        block.tensor(tg)
        mmg = k.cnt["mm"]
        g_mmm[os] = mmg

        def sg(s, os=os, mmg=mmg):
            s.wait_ge(k.sems["mm"], max(0, mmg))
            k.inc(s.copy(out=Bx.g3bc2[:], in_=Bx.pb[4][:, 0:512]), "ev", 1)
        block.scalar(sg)
        g_marks[os] = k.cnt["ev"]

        for tt in range(NTO):
            par = tt % 2
            lp = n_i % 2

            def syd(sy, os=os, tt=tt, par=par, n_i=n_i):
                if n_i == 0:
                    k.w(sy, "h", n_h_all)
                    k.w(sy, "mm")
                if n_i >= 2:
                    sy.wait_ge(k.sems["pe"], max(0, pe0 + (n_i - 1) * 10))
                k.inc(sy.dma_start(out=Bx.hld[par][:],
                                   in_=Bx.h_hbm[tt * 128:(tt + 1) * 128,
                                                os * 512:(os + 1) * 512]), "w", 16)
            block.sync(syd)
            nwl = k.cnt["w"]

            def yem(v, os=os, tt=tt, par=par, nwl=nwl, lp=lp):
                v.wait_ge(k.sems["w"], max(0, nwl))
                v.wait_ge(k.sems["ev"], max(0, g_marks[os]))
                v.scalar_tensor_tensor(out=Bx.lz[lp][:, 0:512], in0=Bx.hld[par][:],
                                       scalar=Bx.gam3[:, tt:tt + 1], in1=Bx.g3bc2[:],
                                       op0=OP.mult, op1=OP.mult)

            def dem(v, lp=lp):
                v.tensor_scalar(out=Bx.k3t[:], in0=Bx.lb[lp][:, 0:512], scalar1=127.0,
                                scalar2=None, op0=OP.min)

            _ladder_emit(k, block, Bx, n_i, lp, 512, yem, dem, pe0, lad0)

            def tr3(t, n_i=n_i):
                t.wait_ge(k.sems["pe"], max(0, pe0 + (n_i + 1) * 10))
                t.wait_ge(k.sems["trv"], max(0, k.cnt["trv"]))
                pp = Bx.pbb[n_i % 2]
                for sub in range(4):
                    inst = t.transpose(pp[:, sub * 128:(sub + 1) * 128],
                                       Bx.k3t[:, sub * 128:(sub + 1) * 128], Bx.identb[:])
                k.inc(inst, "tr", 1)
            block.tensor(tr3)
            ntr = k.cnt["tr"]

            def se3(s, os=os, tt=tt, n_i=n_i, ntr=ntr):
                s.wait_ge(k.sems["tr"], max(0, ntr))
                pp = Bx.pbb[n_i % 2]
                k.inc(s.copy(out=kf3[:, os * 4:(os + 1) * 4, tt * 128:(tt + 1) * 128],
                             in_=pp[:, 0:512].rearrange("p (g c) -> p g c", g=4)),
                      "trv", 1)
            block.scalar(se3)
            n_i += 1

    Bx.n_k3ev = k.cnt["trv"]


def _p5f(k, block, Bx):
    nc = k.nc
    with contextlib.ExitStack() as esf:
        w3b0 = esf.enter_context(nc.sbuf_tensor("w3b0", [128, 16 * 256], F32))
        w3b = [w3b0, w3b0]
        n3 = esf.enter_context(nc.sbuf_tensor("n3_sb", [128, 64 * 256], BF16))
        stage = [esf.enter_context(nc.sbuf_tensor(f"stg5{i}", [128, 256], F32))
                 for i in range(2)]
        kf3 = Bx.k3[:].rearrange("p (o t) -> p o t", o=64)
        n_k3 = Bx.n_k3ev
        acc0 = k.cnt["acc"]
        q30 = k.cnt["q3"]
        ev_marks = {}

        for dg in range(8):
            for qtr in range(4):
                tidx = dg * 4 + qtr

                def syw(sy, dg=dg, qtr=qtr, tidx=tidx):
                    if tidx == 0:
                        k.w(sy, "pe")
                        k.w(sy, "trv", n_k3)
                    if tidx >= 1:
                        k.w(sy, "q3", q30 + tidx)
                    src = Bx.w3t_d[qtr * 2048:(qtr + 1) * 2048,
                                   dg * 256:(dg + 1) * 256]
                    k.inc(sy.dma_start(out=w3b[qtr % 2][:]
                                       .rearrange("p (c n) -> p c n", c=16),
                                       in_=src.rearrange("(c p) n -> p c n", p=128)),
                          "w3", 16)
                block.sync(syw)
                nw = k.cnt["w3"]

                def vt(v, dg=dg, qtr=qtr, nw=nw):
                    v.wait_ge(k.sems["w3"], max(0, nw))
                    if dg >= 1 and qtr == 0:
                        v.wait_ge(k.sems["mm"], max(0, k.cnt["mm"]))
                    dst = n3[:, qtr * 16 * 256:(qtr + 1) * 16 * 256]
                    v.tensor_scalar(out=dst, in0=w3b[qtr % 2][:], scalar1=Bx.t_bc[2][:],
                                    scalar2=None, op0=OP.is_gt)
                    k.inc(v.scalar_tensor_tensor(out=dst, in0=w3b[qtr % 2][:],
                                                 scalar=Bx.nt_bc[2][:], in1=dst,
                                                 op0=OP.is_lt, op1=OP.subtract),
                          "q3", 1)
                block.vector(vt)

            n_tern = k.cnt["q3"]
            for tt in range(NTO):
                par = tt % 2
                idx = dg * 8 + tt

                def tm(t, dg=dg, tt=tt, par=par, idx=idx, n_tern=n_tern):
                    if tt == 0:
                        k.w(t, "q3", n_tern)
                        if dg == 0:
                            k.w(t, "trv", n_k3)
                    if idx >= 2:
                        t.wait_ge(k.sems["ev"], max(0, ev_marks[idx - 2]))
                    for oc in range(64):
                        inst = t.matmul(Bx.pb[par][:, 0:256],
                                        kf3[:, oc, tt * 128:(tt + 1) * 128],
                                        n3[:, oc * 256:(oc + 1) * 256],
                                        start=(oc == 0), stop=(oc == 63))
                    k.inc(inst, "mm", 1)
                block.tensor(tm)
                nmm = k.cnt["mm"]

                def sv(s, dg=dg, tt=tt, par=par, idx=idx, nmm=nmm):
                    s.wait_ge(k.sems["mm"], max(0, nmm))
                    if idx >= 2:
                        s.wait_ge(k.sems["acc"], max(0, acc0 + 16 * (idx - 1)))
                    else:
                        s.wait_ge(k.sems["acc"], max(0, acc0))
                    k.inc(s.activation(out=stage[par][:, 0:256],
                                       in_=Bx.pb[par][:, 0:256], func=AF.Identity,
                                       scale=Bx.sig3[:, tt:tt + 1]), "ev", 1)
                block.scalar(sv)
                ev_marks[idx] = k.cnt["ev"]
                nev = k.cnt["ev"]

                def ga(g, dg=dg, tt=tt, par=par, nev=nev):
                    g.wait_ge(k.sems["ev"], max(0, nev))
                    k.inc(g.dma_start(out=Bx.out_d[tt * 128:(tt + 1) * 128,
                                                   dg * 256:(dg + 1) * 256],
                                      in_=stage[par][:, 0:256], accum_op=OP.add),
                          "acc", 16)
                block.gpsimd(ga)


def kernel(L0_attn_hidden_states, L0_attn_Wq, L0_attn_Wk, L0_attn_Wv, L0_attn_Wo,
           L0_ffn_w1, L0_ffn_g1, L0_ffn_w2, L0_ffn_g2, L0_ffn_w3, L0_ffn_g3):
    x = np.ascontiguousarray(np.asarray(L0_attn_hidden_states, dtype=np.float32))
    wq = np.ascontiguousarray(np.asarray(L0_attn_Wq, dtype=np.float32))
    wk = np.ascontiguousarray(np.asarray(L0_attn_Wk, dtype=np.float32))
    wv = np.ascontiguousarray(np.asarray(L0_attn_Wv, dtype=np.float32))
    wo = np.ascontiguousarray(np.asarray(L0_attn_Wo, dtype=np.float32))
    w1t = np.ascontiguousarray(np.asarray(L0_ffn_w1, dtype=np.float32).T)
    w2t = np.ascontiguousarray(np.asarray(L0_ffn_w2, dtype=np.float32).T)
    w3t = np.ascontiguousarray(np.asarray(L0_ffn_w3, dtype=np.float32).T)
    g1 = np.asarray(L0_ffn_g1, dtype=np.float32)
    g2 = np.asarray(L0_ffn_g2, dtype=np.float32)
    g3 = np.asarray(L0_ffn_g3, dtype=np.float32)
    g1f = np.ascontiguousarray(g1.reshape(NC, 128).T)
    g2f = np.ascontiguousarray(g2.reshape(NC, 128).T)
    g1r = np.ascontiguousarray(g1.reshape(1, D))
    g2r = np.ascontiguousarray(g2.reshape(1, D))
    g3r = np.ascontiguousarray(g3.reshape(1, HF))

    nc = build()
    in_maps = []
    for core in range(8):
        b = core // 2
        half = core % 2
        own = x[b, half * TO:(half + 1) * TO, :]
        other = x[b, (1 - half) * TO:(2 - half) * TO, :]
        x_local = np.ascontiguousarray(np.concatenate([own, other], axis=0))
        x_fm4 = np.ascontiguousarray(
            x_local.reshape(NTA, 128, NC, 128).transpose(0, 2, 3, 1))
        pos = np.concatenate(
            [np.arange(half * TO, (half + 1) * TO, dtype=np.int32),
             np.arange((1 - half) * TO, (2 - half) * TO, dtype=np.int32)])
        posf = np.ascontiguousarray(pos.reshape(NTA, 128).T)
        in_maps.append({
            "x_tm": x_local, "x_fm4": x_fm4,
            "wq": wq, "wk": wk, "wv": wv, "wo": wo,
            "w1t": w1t, "w2t": w2t, "w3t": w3t,
            "g1f": g1f, "g2f": g2f, "g1r": g1r, "g2r": g2r, "g3r": g3r,
            "posf": posf,
        })
    res = run_bass_kernel_spmd(nc, in_maps, core_ids=list(range(8)))
    out = np.empty((4, TA, D), dtype=np.float32)
    for core in range(8):
        b = core // 2
        half = core % 2
        out[b, half * TO:(half + 1) * TO, :] = res.results[core]["out"]
    return out


# version 03 (final)
# speedup vs baseline: 1.0074x; 1.0074x over previous
import math
import contextlib
import numpy as np

import concourse.bass as bass
import concourse.mybir as mybir
from concourse.alu_op_type import AluOpType as OP
from concourse.bass_utils import run_bass_kernel_spmd

F32 = mybir.dt.float32
BF16 = mybir.dt.bfloat16
I32 = mybir.dt.int32
AF = mybir.ActivationFunctionType
AX = mybir.AxisListType.X


def GG(v, n=2):
    for _ in range(n):
        v.engine_nop()

D = 2048
TO = 1024
TA = 2048
HF = 8192
NC = 16
NTO = 8
NTA = 16
EPS_LN = 1e-5
EPS_RMS = 1e-8
EPS_T = 1e-8
SCALE_ATT = float(np.float32(1.0) / np.float32(np.sqrt(np.float32(128.0))))
CA_ATT = 0.2972450852394104
CB_ATT = 0.29735848307609558
C1_SIG = 0.0078740119934082031
C2_SIG = -0.5000002384185791
BIG = float(np.float32(2.0 ** 100))
C2PI_1 = float(np.float32(2 * math.pi))
C2PI_2 = float(np.float32(2 * math.pi - float(np.float32(2 * math.pi))))
INV2PI = float(np.float32(1.0 / (2 * math.pi)))
PIH_1 = float(np.float32(math.pi / 2))
PIH_2 = float(np.float32(math.pi / 2 - float(np.float32(math.pi / 2))))
PI_F32 = float(np.float32(math.pi))
PI_LO = float(np.nextafter(np.float32(math.pi), np.float32(0.0)))
LVLS = [128.0, 64.0, 32.0, 16.0, 8.0, 4.0, 2.0, 1.0]


class K:
    def __init__(self):
        self.nc = bass.Bass("TRN2", target_bir_lowering=False,
                            detect_race_conditions=False)
        self.es = contextlib.ExitStack()
        self.sems = {}
        self.cnt = {}
        self.xld = [0, 0]

    def sem(self, *names):
        for n in names:
            self.sems[n] = self.es.enter_context(self.nc.semaphore(n))
            self.cnt[n] = 0

    def inc(self, inst, name, k=1):
        inst.then_inc(self.sems[name], k)
        self.cnt[name] += k
        return self.cnt[name]

    def w(self, eng, name, val=None):
        eng.wait_ge(self.sems[name],
                    max(0, self.cnt[name] if val is None else val))

    def sb(self, name, shape, dt):
        return self.es.enter_context(self.nc.sbuf_tensor(name, shape, dt))

    def psum(self, name, shape, dt):
        return self.es.enter_context(self.nc.psum_tensor(name, shape, dt))


class B:
    pass


def _ln_apply_fm(k, v, Bx, i, par, width=D):
    nchunk = NC
    x3 = Bx.xin[par][:, 0:width].rearrange("p (c j) -> p c j", j=128)
    xo3 = Bx.xnb[par][:, 0:width].rearrange("p (c j) -> p c j", j=128)
    nch = width // 128
    mb3 = Bx.m_bc[:, i * 128:(i + 1) * 128].unsqueeze(1).broadcast_to([128, nch, 128])
    rb3 = Bx.r_bc[:, i * 128:(i + 1) * 128].unsqueeze(1).broadcast_to([128, nch, 128])
    v.tensor_tensor(out=xo3, in0=x3, in1=mb3, op=OP.subtract)
    v.tensor_tensor(out=xo3, in0=xo3, in1=rb3, op=OP.mult)


def _rope(k, v, Bx, tile_idx, buf):
    b3 = buf[:].rearrange("p (g c) -> p g c", g=4)
    x0 = b3[:, :, 0:64:2]
    x1 = b3[:, :, 1:64:2]
    co = Bx.cos_sb[:, tile_idx * 32:(tile_idx + 1) * 32].unsqueeze(1).broadcast_to([128, 4, 32])
    si = Bx.sin_sb[:, tile_idx * 32:(tile_idx + 1) * 32].unsqueeze(1).broadcast_to([128, 4, 32])
    o3 = Bx.ohtmp[:].rearrange("p (g c) -> p g c", g=4)
    s0, s1, s2, s3 = o3[:, :, 0:32], o3[:, :, 32:64], o3[:, :, 64:96], o3[:, :, 96:128]
    v.tensor_tensor(out=s0, in0=x0, in1=co, op=OP.mult)
    v.tensor_tensor(out=s1, in0=x1, in1=si, op=OP.mult)
    v.tensor_tensor(out=s2, in0=x0, in1=si, op=OP.mult)
    v.tensor_tensor(out=s3, in0=x1, in1=co, op=OP.mult)
    v.tensor_tensor(out=x0, in0=s0, in1=s1, op=OP.subtract)
    v.tensor_tensor(out=x1, in0=s2, in1=s3, op=OP.add)


def _bcast_row(k, block, Bx, row_ap, dest, ncols):
    nslab = (ncols + 511) // 512

    def te(t):
        k.w(t, "ev")
        for j in range(nslab):
            w0, w1 = j * 512, min(ncols, j * 512 + 512)
            k.inc(t.matmul(Bx.pb[j % 4][:, 0:w1 - w0], Bx.ones1[:],
                           row_ap[0:1, w0:w1], start=True, stop=True), "mm", 1)
    block.tensor(te)

    def sc(s):
        k.w(s, "mm")
        for j in range(nslab):
            w0, w1 = j * 512, min(ncols, j * 512 + 512)
            k.inc(s.copy(out=dest[:, w0:w1], in_=Bx.pb[j % 4][:, 0:w1 - w0]), "ev", 1)
    block.scalar(sc)


def _ladder_emit(k, block, Bx, n_idx, lp, Fd, y_emit, dest_emit, pe0, lad0):
    lz = Bx.lz[lp][:, 0:Fd]
    lr = Bx.lr[lp][:, 0:Fd]
    lb = Bx.lb[lp][:, 0:Fd]

    def vsec(v):
        y_emit(v)
        v.tensor_scalar(out=lz, in0=lz, scalar1=128.5, scalar2=None, op0=OP.add)
        k.inc(v.tensor_copy(out=lr, in_=lz), "pe", 1)
        for j, c in enumerate(LVLS):
            v.wait_ge(k.sems["lad"], max(0, lad0 + n_idx * 8 + j + 1))
            k.inc(v.scalar_tensor_tensor(out=lr, in0=lb, scalar=-c, in1=lr,
                                         op0=OP.mult, op1=OP.add), "pe", 1)
        v.scalar_tensor_tensor(out=lb, in0=lr, scalar=-1.0, in1=lz,
                               op0=OP.mult, op1=OP.add)
        v.tensor_scalar(out=lb, in0=lb, scalar1=-128.0, scalar2=-128.0,
                        op0=OP.add, op1=OP.max)
        dest_emit(v)
        k.inc(v.engine_nop(), "pe", 1)
    block.vector(vsec)

    def ssec(s):
        for j in range(8):
            s.wait_ge(k.sems["pe"], max(0, pe0 + n_idx * 10 + j + 1))
            k.inc(s.activation(out=lb, in_=lr, func=AF.Sigmoid,
                               scale=Bx.big_ap[:], bias=Bx.lvb[j][:]), "lad", 1)
    block.scalar(ssec)


P2PARTS = ("mm", "tr")


NOACC = False


def build(stages="all", p2parts=("mm", "tr"), noacc=False):
    global NOACC
    NOACC = noacc
    global P2PARTS
    P2PARTS = p2parts
    k = K()
    nc = k.nc
    dram = lambda n, s, dt, kind: nc.dram_tensor(n, s, dt, kind=kind).ap()
    B.x_tm = dram("x_tm", [TA, D], F32, "ExternalInput")
    B.x_fm4 = dram("x_fm4", [NTA, NC, 128, 128], F32, "ExternalInput")
    B.wq_d = dram("wq", [D, D], F32, "ExternalInput")
    B.wk_d = dram("wk", [D, 512], F32, "ExternalInput")
    B.wv_d = dram("wv", [D, 512], F32, "ExternalInput")
    B.wo_d = dram("wo", [D, D], F32, "ExternalInput")
    B.w1t_d = dram("w1t", [D, HF], F32, "ExternalInput")
    B.w2t_d = dram("w2t", [D, HF], F32, "ExternalInput")
    B.w3t_d = dram("w3t", [HF, D], F32, "ExternalInput")
    B.g1f_d = dram("g1f", [128, NC], F32, "ExternalInput")
    B.g2f_d = dram("g2f", [128, NC], F32, "ExternalInput")
    B.g1r_d = dram("g1r", [1, D], F32, "ExternalInput")
    B.g2r_d = dram("g2r", [1, D], F32, "ExternalInput")
    B.g3r_d = dram("g3r", [1, HF], F32, "ExternalInput")
    B.pos_d = dram("posf", [128, NTA], I32, "ExternalInput")
    B.h_hbm = dram("h_hbm", [TO, HF], F32,
                   "ExternalOutput" if stages == "alld" else "Internal")
    B.fl_hbm = dram("fl_hbm", [48, 128], F32, "Internal")
    B.out_d = dram("out", [TO, D], F32, "ExternalOutput")
    if stages in ("p3w1d", "p0d", "alld"):
        B.dbg_d = dram("dbg", [1408, 8256], F32, "ExternalOutput")

    k.sem("c", "xe", "xo", "w", "w2", "w3", "mm", "ev", "ro", "tr", "trv",
          "ln", "exp", "av", "avv", "st", "acc", "h", "lad", "pe", "q3",
          "b1", "b2", "b3", "b4", "wm")

    B.ident = k.sb("ident", [128, 128], F32)
    B.identb = k.sb("identb", [128, 128], BF16)
    B.ones1 = k.sb("ones1", [1, 128], F32)
    B.onesc = k.sb("onesc", [128, 1], F32)
    B.g1f = k.sb("g1f_s", [128, NC], F32)
    B.g2f = k.sb("g2f_s", [128, NC], F32)
    B.g3s = k.sb("g3s", [1, 512], F32)
    B.m_tm = k.sb("m_tm", [128, NTA], F32)
    B.var_tm = k.sb("var_tm", [128, NTA], F32)
    B.r_tm = k.sb("r_tm", [128, NTA], F32)
    B.stat48 = k.sb("stat48", [128, 48], F32)
    B.rm1 = k.sb("rm1", [128, NTO], F32)
    B.rm2 = k.sb("rm2", [128, NTO], F32)
    B.isc1 = k.sb("isc1", [128, NTO], F32)
    B.isc2 = k.sb("isc2", [128, NTO], F32)
    B.sig1 = k.sb("sig1", [128, NTO], F32)
    B.sig2 = k.sb("sig2", [128, NTO], F32)
    B.ssq3 = k.sb("ssq3", [128, NTO * 16], F32)
    B.rm3c = k.sb("rm3c", [128, NTO * 16], F32)
    B.gam3 = k.sb("gam3", [128, NTO], F32)
    B.sig3 = k.sb("sig3", [128, NTO], F32)
    B.stsc = k.sb("stsc", [128, 24], F32)
    B.flip48 = k.sb("flip48", [48, 128], F32)
    B.row_s = k.sb("row_s", [1, TA], F32)
    B.big_ap = k.sb("big_ap", [128, 1], F32)
    B.lvb = [k.sb(f"lvb{i}", [128, 1], F32) for i in range(8)]
    B.t_bc = [k.sb(f"t_bc{i}", [128, 1], F32) for i in range(3)]
    B.nt_bc = [k.sb(f"nt_bc{i}", [128, 1], F32) for i in range(3)]
    B.a3_bc = k.sb("a3_bc", [128, 1], F32)
    B.sc11 = k.sb("sc11", [1, 16], F32)
    B.a_sc = k.sb("a_sc", [1, 16], F32)
    B.bn24 = k.sb("bn24", [128, 24], F32)
    B.st2 = k.sb("st2", [128, 2], F32)
    B.nb1 = k.sb("nb1", [128, 1], F32)
    B.io_p = k.sb("io_p", [128, 1], I32)
    B.io_pf = k.sb("io_pf", [128, 1], F32)
    B.pos_s = k.sb("pos_s", [128, NTA], I32)
    B.ohtmp = k.sb("ohtmp", [128, 512], F32)
    B.rcp4 = k.sb("rcp4", [128, 4], F32)
    B.rn4 = k.sb("rn4", [128, 4], F32)

    B.pb = [k.psum(f"pb{i}", [128, 512], F32) for i in range(6)]
    B.pbb = [k.psum(f"pbb{i}", [128, 1024], BF16) for i in range(2)]

    with k.es:
        with nc.Block() as block:
            with contextlib.ExitStack() as scA:
                sba = lambda n, s, dt: scA.enter_context(nc.sbuf_tensor(n, s, dt))
                B.cos_sb = sba("cos_sb", [128, NTA * 32], F32)
                B.sin_sb = sba("sin_sb", [128, NTA * 32], F32)
                B.m_bc = sba("m_bc", [128, TA], F32)
                B.r_bc = sba("r_bc", [128, TA], F32)
                B.xin = [sba(f"xin{i}", [128, D], F32) for i in range(2)]
                B.xnb = [sba(f"xnb{i}", [128, D], F32) for i in range(2)]
                B.ktmp = sba("ktmp", [128, 512], F32)
                with contextlib.ExitStack() as sc0:
                    sb0 = lambda n, s, dt: sc0.enter_context(nc.sbuf_tensor(n, s, dt))
                    B.io_f = sb0("io_f", [128, 128], I32)
                    B.io_ff = sb0("io_ff", [128, 128], F32)
                    B.jrow = sb0("jrow", [1, 32], I32)
                    B.jrowf = sb0("jrowf", [1, 32], F32)
                    B.frow = sb0("frow", [1, 32], F32)
                    B.f_bc = sb0("f_bc", [128, 32], F32)
                    B.posff = sb0("posff", [128, NTA], F32)
                    B.rpk512 = sb0("rpk512", [128, 512], I32)
                    _p0(k, block, B)
                    if stages == "p0d":
                        def p0_dump(sy):
                            k.w(sy, "b3")
                            k.inc(sy.dma_start(out=B.dbg_d[1152:1280, 0:512],
                                               in_=B.cos_sb[:]), "st", 16)
                            k.inc(sy.dma_start(out=B.dbg_d[1280:1408, 0:512],
                                               in_=B.sin_sb[:]), "st", 16)
                            k.inc(sy.dma_start(out=B.dbg_d[0:128, 0:32],
                                               in_=B.f_bc[:]), "st", 16)
                            k.inc(sy.dma_start(out=B.dbg_d[0:128, 32:48],
                                               in_=B.posff[:]), "st", 16)
                            k.inc(sy.dma_start(out=B.dbg_d[128:129, 0:32],
                                               in_=B.frow[:]), "st", 16)
                        block.sync(p0_dump)
                if stages not in ("p0", "p0d"):
                    _p1(k, block, B)
                with contextlib.ExitStack() as scKV:
                    sbk = lambda n, s, dt: scKV.enter_context(nc.sbuf_tensor(n, s, dt))
                    B.k_fm = sbk("k_fm", [128, 4 * TA], F32)
                    B.v_sb = sbk("v_sb", [128, NTA * 516], F32)
                    if stages not in ("p0", "p0d", "p1"):
                        _p2(k, block, B)
                    if stages in ("p0", "p0d", "p1", "p2"):
                        raise_skip = True
                    with contextlib.ExitStack() as sc3:
                        sb3 = lambda n, s, dt: sc3.enter_context(nc.sbuf_tensor(n, s, dt))
                        B.wqo = sb3("wqo", [128, NC * 512], F32)
                        B.ptile = [sb3(f"ptile{i}", [128, 512], F32) for i in range(2)]
                        B.q_fm = sb3("q_fm", [128, 4 * TO], F32)
                        B.oh_fm = sb3("oh_fm", [128, 4 * TO], F32)
                        B.wos = sb3("wos", [128, 4 * 256], F32)
                        if stages in ("all", "alld", "p3", "p5", "p3q", "p3qa", "p3a1", "p3a2", "p3a4", "p3w1", "p3w2", "p3w2q", "p3w2a", "p3w2a1", "p3w3", "p3w4q", "p3w4a", "p3w1d"):
                            for Q in range(4):
                                _p3_qproj(k, block, B, Q)
                                if stages == "p3w2q" and Q == 1:
                                    break
                                if stages == "p3w4q" and Q == 3:
                                    break
                                if stages == "p3q":
                                    break
                                if stages == "p3a1":
                                    _p3_attn(k, block, B, Q, 1, 1)
                                    break
                                if stages == "p3a2":
                                    _p3_attn(k, block, B, Q, 1, 2)
                                    break
                                if stages == "p3w2a1":
                                    if Q == 0:
                                        _p3_attn(k, block, B, Q)
                                        _p3_wo(k, block, B, Q)
                                        continue
                                    _p3_attn(k, block, B, Q, 1, 1)
                                    break
                                if stages == "p3a4":
                                    _p3_attn(k, block, B, Q, 2, 2)
                                    break
                                _p3_attn(k, block, B, Q)
                                if stages == "p3qa":
                                    break
                                if stages == "p3w1d" and Q == 0:
                                    def dbg_dump(sy):
                                        k.w(sy, "trv")
                                        k.inc(sy.dma_start(out=B.dbg_d[0:128, 0:8192],
                                                           in_=B.k_fm[:]), "st", 16)
                                        k.inc(sy.dma_start(out=B.dbg_d[128:256, 0:4096],
                                                           in_=B.q_fm[:]), "st", 16)
                                        k.inc(sy.dma_start(out=B.dbg_d[256:384, 0:8256],
                                                           in_=B.v_sb[:]), "st", 16)
                                        k.inc(sy.dma_start(out=B.dbg_d[384:512, 0:4096],
                                                           in_=B.oh_fm[:]), "st", 16)
                                        k.inc(sy.dma_start(out=B.dbg_d[512:640, 0:2048],
                                                           in_=B.m_bc[:]), "st", 16)
                                        k.inc(sy.dma_start(out=B.dbg_d[640:768, 0:2048],
                                                           in_=B.r_bc[:]), "st", 16)
                                        k.inc(sy.dma_start(out=B.dbg_d[768:816, 0:128],
                                                           in_=B.flip48[:]), "st", 16)
                                        k.inc(sy.dma_start(out=B.dbg_d[816:944, 0:48],
                                                           in_=B.stat48[:]), "st", 16)
                                        k.inc(sy.dma_start(out=B.dbg_d[944:1072, 0:16],
                                                           in_=B.m_tm[:]), "st", 16)
                                        k.inc(sy.dma_start(out=B.dbg_d[944:1072, 16:32],
                                                           in_=B.var_tm[:]), "st", 16)
                                        k.inc(sy.dma_start(out=B.dbg_d[944:1072, 32:48],
                                                           in_=B.r_tm[:]), "st", 16)
                                        k.inc(sy.dma_start(out=B.dbg_d[1152:1280, 0:512],
                                                           in_=B.cos_sb[:]), "st", 16)
                                        k.inc(sy.dma_start(out=B.dbg_d[1280:1408, 0:512],
                                                           in_=B.sin_sb[:]), "st", 16)
                                    block.sync(dbg_dump)
                                if stages == "p3w2a" and Q == 1:
                                    break
                                if stages == "p3w4a" and Q == 3:
                                    break
                                _p3_wo(k, block, B, Q)
                                if stages in ("p3w1", "p3w1d"):
                                    break
                                if stages == "p3w2" and Q == 1:
                                    break
                                if stages == "p3w3" and Q == 2:
                                    break
                if stages in ("all", "alld", "p5"):
                    _p5a(k, block, B)
                    with contextlib.ExitStack() as scBC:
                        sbb = lambda n, s, dt: scBC.enter_context(nc.sbuf_tensor(n, s, dt))
                        B.k1 = sbb("k1_fm", [128, NC * TO], BF16)
                        B.k2 = sbb("k2_fm", [128, NC * TO], BF16)
                        with contextlib.ExitStack() as scL1:
                            sbl = lambda n, s, dt: scL1.enter_context(nc.sbuf_tensor(n, s, dt))
                            B.lz = [sbl(f"lz{i}", [128, 1024], F32) for i in range(2)]
                            B.lr = [sbl(f"lr{i}", [128, 1024], F32) for i in range(2)]
                            B.lb = [sbl(f"lb{i}", [128, 1024], F32) for i in range(2)]
                            _p5b(k, block, B)
                        _p5c(k, block, B)
            if stages in ("all", "alld", "p5"):
                _p5d(k, block, B)
                with contextlib.ExitStack() as scE:
                    sbe = lambda n, s, dt: scE.enter_context(nc.sbuf_tensor(n, s, dt))
                    B.k3 = sbe("k3_fm", [128, 64 * TO], BF16)
                    with contextlib.ExitStack() as scE2:
                        sb2 = lambda n, s, dt: scE2.enter_context(nc.sbuf_tensor(n, s, dt))
                        B.lz = [sb2(f"lz3{i}", [128, 512], F32) for i in range(2)]
                        B.lr = [sb2(f"lr3{i}", [128, 512], F32) for i in range(2)]
                        B.lb = [sb2(f"lb3{i}", [128, 512], F32) for i in range(2)]
                        B.g3bc2 = sb2("g3bc2", [128, 512], F32)
                        B.hld = [sb2(f"hld{i}", [128, 512], F32) for i in range(2)]
                        B.k3t = sb2("k3t", [128, 512], BF16)
                        _p5e(k, block, B)
                    _p5f(k, block, B)

            if stages == "alld":
                def p5_dump(sy):
                    for sem in ("acc", "ev", "b4", "trv", "av"):
                        k.w(sy, sem)
                    dmp = [(B.a_sc[0:1, 0:16], 0, 0, 1, 16),
                           (B.sc11[0:1, 0:16], 1, 0, 1, 16),
                           (B.sig1[:], 2, 0, 128, NTO),
                           (B.sig2[:], 2, 16, 128, NTO),
                           (B.isc1[:], 2, 32, 128, NTO),
                           (B.isc2[:], 2, 48, 128, NTO),
                           (B.gam3[:], 2, 64, 128, NTO),
                           (B.sig3[:], 2, 80, 128, NTO),
                           (B.stat48[:], 2, 96, 128, 48),
                           (B.ssq3[:], 2, 144, 128, NTO * 16),
                           (B.rm3c[:], 2, 272, 128, NTO * 16),
                           (B.t_bc[2][:], 2, 400, 128, 1),
                           (B.nt_bc[2][:], 2, 401, 128, 1),
                           (B.a3_bc[:], 2, 402, 128, 1)]
                    for src, row, col, nr, ncol in dmp:
                        k.inc(sy.dma_start(
                            out=B.dbg_d[row * 128:row * 128 + nr, col:col + ncol],
                            in_=src, allow_slow_non_contiguous=True), "st", 16)
                block.sync(p5_dump)

            n_acc, n_st, n_h = k.cnt["acc"], k.cnt["st"], k.cnt["h"]

            def gfin(g):
                g.wait_ge(k.sems["acc"], max(0, n_acc))
            block.gpsimd(gfin)

            def syfin(sy):
                sy.wait_ge(k.sems["st"], max(0, n_st))
                sy.wait_ge(k.sems["h"], max(0, n_h))
                sy.wait_ge(k.sems["acc"], max(0, n_acc))
            block.sync(syfin)
    return nc


def _p0(k, block, Bx):
    def sy(sy):
        k.inc(sy.dma_start(out=Bx.g1f[:], in_=Bx.g1f_d), "c", 16)
        k.inc(sy.dma_start(out=Bx.g2f[:], in_=Bx.g2f_d), "c", 16)
        k.inc(sy.dma_start(out=Bx.pos_s[:], in_=Bx.pos_d), "c", 16)
    block.sync(sy)
    n_c0 = k.cnt["c"]

    def gp(g):
        k.inc(g.iota(out=Bx.io_p[:], pattern=[[1, 1]], base=0, channel_multiplier=1),
              "b2", 1)
        k.inc(g.iota(out=Bx.io_f[:], pattern=[[1, 128]], base=0, channel_multiplier=0),
              "b2", 1)
        k.inc(g.iota(out=Bx.jrow[:], pattern=[[1, 32]], base=0, channel_multiplier=0),
              "b2", 1)
    block.gpsimd(gp)

    def vid(v):
        k.w(v, "b2", 3)
        v.tensor_copy(out=Bx.io_pf[:], in_=Bx.io_p[:])
        v.tensor_copy(out=Bx.io_ff[:], in_=Bx.io_f[:])
        GG(v)
        v.tensor_scalar(out=Bx.ident[:], in0=Bx.io_ff[:], scalar1=Bx.io_pf[:],
                        scalar2=None, op0=OP.is_equal)
        k.inc(v.tensor_scalar(out=Bx.identb[:], in0=Bx.io_ff[:], scalar1=Bx.io_pf[:],
                              scalar2=None, op0=OP.is_equal), "b1", 1)
    block.vector(vid)

    def ve(v):
        v.memset(Bx.ones1[:], 1.0)
        v.memset(Bx.onesc[:], 1.0)
        v.memset(Bx.big_ap[:], BIG)
        for i, c in enumerate(LVLS):
            v.memset(Bx.lvb[i][:], -BIG * c)
        k.w(v, "c", n_c0)
        v.tensor_copy(out=Bx.jrowf[:], in_=Bx.jrow[:])
        k.inc(v.tensor_copy(out=Bx.posff[:], in_=Bx.pos_s[:]), "b1", 1)
    block.vector(ve)

    def vc10k(v):
        k.inc(v.memset(Bx.sc11[0:1, 0:1], 10000.0), "b1", 1)
    block.vector(vc10k)

    def sc(s):
        k.w(s, "b1", 3)
        k.inc(s.activation(out=Bx.sc11[0:1, 1:2], in_=Bx.sc11[0:1, 0:1], func=AF.Ln),
              "b2", 1)
    block.scalar(sc)

    def ve2(v):
        k.w(v, "b2")
        k.inc(v.tensor_scalar(out=Bx.jrowf[:], in0=Bx.jrowf[:],
                              scalar1=Bx.sc11[0:1, 1:2], scalar2=-1.0 / 32.0,
                              op0=OP.mult, op1=OP.mult), "b2", 1)
    block.vector(ve2)

    def sc2(s):
        k.w(s, "b2")
        k.inc(s.activation(out=Bx.frow[:], in_=Bx.jrowf[:], func=AF.Exp), "b2", 1)
    block.scalar(sc2)

    def te(t):
        k.w(t, "b2")
        k.inc(t.matmul(Bx.pb[5][:, 0:32], Bx.ones1[:], Bx.frow[:], start=True,
                       stop=True), "mm", 1)
    block.tensor(te)

    def sc3(s):
        k.w(s, "mm")
        k.inc(s.copy(out=Bx.f_bc[:], in_=Bx.pb[5][:, 0:32]), "b2", 1)
    block.scalar(sc3)
    n_b2 = k.cnt["b2"]

    def ve3(v):
        k.w(v, "b2")

        TH = Bx.xnb[0][:, 0:512]
        Y = Bx.xnb[0][:, 512:1024]
        KF = Bx.xnb[0][:, 1024:1536]
        FLG = Bx.xnb[0][:, 1536:2048]
        f_b = Bx.f_bc[:].unsqueeze(1).broadcast_to([128, NTA, 32])
        p_b = Bx.posff[:].unsqueeze(2).broadcast_to([128, NTA, 32])

        def wrap_clamp(dst):
            v.tensor_scalar(out=FLG, in0=Y, scalar1=PI_F32,
                            scalar2=None, op0=OP.is_gt)
            v.scalar_tensor_tensor(out=FLG, in0=Y, scalar=-PI_F32,
                                   in1=FLG, op0=OP.is_lt, op1=OP.subtract)
            v.scalar_tensor_tensor(out=Y, in0=FLG, scalar=C2PI_1,
                                   in1=Y, op0=OP.mult, op1=OP.add)
            v.scalar_tensor_tensor(out=Y, in0=FLG, scalar=C2PI_2,
                                   in1=Y, op0=OP.mult, op1=OP.add)
            v.tensor_scalar(out=dst, in0=Y, scalar1=PI_LO, scalar2=-PI_LO,
                            op0=OP.min, op1=OP.max)

        v.tensor_tensor(out=TH.rearrange("p (t j) -> p t j", j=32), in0=f_b,
                        in1=p_b, op=OP.mult)
        v.tensor_scalar(out=Y, in0=TH, scalar1=INV2PI, scalar2=None, op0=OP.mult)
        v.tensor_copy(out=Bx.rpk512[:], in_=Y)
        v.tensor_copy(out=KF, in_=Bx.rpk512[:])
        v.scalar_tensor_tensor(out=Y, in0=KF, scalar=-C2PI_1,
                               in1=TH, op0=OP.mult, op1=OP.add)
        v.scalar_tensor_tensor(out=Y, in0=KF, scalar=-C2PI_2,
                               in1=Y, op0=OP.mult, op1=OP.add)
        wrap_clamp(Bx.sin_sb[:])
        v.tensor_scalar(out=Y, in0=TH, scalar1=INV2PI, scalar2=0.25,
                        op0=OP.mult, op1=OP.add)
        v.tensor_copy(out=Bx.rpk512[:], in_=Y)
        v.tensor_copy(out=KF, in_=Bx.rpk512[:])
        v.scalar_tensor_tensor(out=Y, in0=KF, scalar=-C2PI_1,
                               in1=TH, op0=OP.mult, op1=OP.add)
        v.scalar_tensor_tensor(out=Y, in0=KF, scalar=-C2PI_2,
                               in1=Y, op0=OP.mult, op1=OP.add)
        v.tensor_scalar(out=Y, in0=Y,
                        scalar1=PIH_1, scalar2=PIH_2, op0=OP.add, op1=OP.add)
        wrap_clamp(Bx.cos_sb[:])
        k.inc(v.engine_nop(), "b3", 1)
    block.vector(ve3)

    def sc4(s):
        k.w(s, "b3", 1)
        k.inc(s.activation(out=Bx.sin_sb[:], in_=Bx.sin_sb[:], func=AF.Sin), "b3", 1)
        k.inc(s.activation(out=Bx.cos_sb[:], in_=Bx.cos_sb[:], func=AF.Sin), "b3", 1)
    block.scalar(sc4)


def _p1(k, block, Bx):
    nc = k.nc
    with contextlib.ExitStack() as es1:
        g1bc = es1.enter_context(nc.sbuf_tensor("g1bc", [128, D], F32))
        g2bc = es1.enter_context(nc.sbuf_tensor("g2bc", [128, D], F32))

        def syg1(sy):
            k.inc(sy.dma_start(out=Bx.row_s[:, 0:D], in_=Bx.g1r_d), "w", 16)
        block.sync(syg1)
        n_wg1 = k.cnt["w"]

        def tw(t):
            k.w(t, "c")
            k.w(t, "b3")
            k.w(t, "b1")
            k.w(t, "w", n_wg1)
        block.tensor(tw)
        _bcast_row(k, block, Bx, Bx.row_s, g1bc, D)
        n_mm_g1 = k.cnt["mm"]

        def syg2(sy):
            k.w(sy, "mm", n_mm_g1)
            k.inc(sy.dma_start(out=Bx.row_s[:, 0:D], in_=Bx.g2r_d), "w", 16)
        block.sync(syg2)
        n_wg2 = k.cnt["w"]

        def tw2a(t):
            k.w(t, "w", n_wg2)
        block.tensor(tw2a)
        _bcast_row(k, block, Bx, Bx.row_s, g2bc, D)
        n_ev_g = k.cnt["ev"]

        ln0 = k.cnt["ln"]
        for i in range(NTA):
            par = i % 2
            pn = "xe" if par == 0 else "xo"

            def syd(sy, i=i, par=par, pn=pn):
                if i >= 2:
                    k.w(sy, "ln", ln0 + i - 1)
                k.xld[par] += 1
                k.inc(sy.dma_start(out=Bx.xin[par][:],
                                   in_=Bx.x_tm[i * 128:(i + 1) * 128, :]), pn, 16)
            block.sync(syd)

            def ve(v, i=i, par=par, pn=pn):
                if i == 0:
                    k.w(v, "ev", n_ev_g)
                v.wait_ge(k.sems[pn], 16 * k.xld[par])
                xt = Bx.xin[par]
                for q in range(4):
                    v.bn_stats(out=Bx.bn24[:, q * 6:(q + 1) * 6],
                               in_=xt[:, q * 512:(q + 1) * 512])
                GG(v)
                v.bn_aggr(out=Bx.st2[:], in_=Bx.bn24[:])
                GG(v)
                v.tensor_copy(out=Bx.m_tm[:, i:i + 1], in_=Bx.st2[:, 0:1])
                v.tensor_copy(out=Bx.var_tm[:, i:i + 1], in_=Bx.st2[:, 1:2])
                k.inc(v.tensor_scalar(out=Bx.stsc[:, 0:1], in0=Bx.st2[:, 1:2],
                                      scalar1=EPS_LN, scalar2=None, op0=OP.add),
                      "b4", 1)
            block.vector(ve)

            def sq(s, i=i):
                k.w(s, "b4")
                k.inc(s.activation(out=Bx.stsc[:, 1:2], in_=Bx.stsc[:, 0:1],
                                   func=AF.Sqrt), "b4", 1)
            block.scalar(sq)

            def vn(v, i=i):
                k.w(v, "b4")
                v.reciprocal(out=Bx.stsc[:, 2:3], in_=Bx.stsc[:, 1:2])
                GG(v)
                v.tensor_mul(out=Bx.stsc[:, 3:4], in0=Bx.stsc[:, 2:3],
                             in1=Bx.stsc[:, 2:3])
                GG(v)
                v.tensor_mul(out=Bx.stsc[:, 3:4], in0=Bx.stsc[:, 3:4],
                             in1=Bx.stsc[:, 0:1])
                GG(v)
                v.tensor_scalar(out=Bx.stsc[:, 3:4], in0=Bx.stsc[:, 3:4],
                                scalar1=-0.5, scalar2=1.5, op0=OP.mult, op1=OP.add)
                GG(v)
                v.tensor_mul(out=Bx.r_tm[:, i:i + 1], in0=Bx.stsc[:, 2:3],
                             in1=Bx.stsc[:, 3:4])
                if i < NTO:
                    GG(v)
                    v.tensor_mul(out=Bx.nb1[:], in0=Bx.m_tm[:, i:i + 1],
                                 in1=Bx.r_tm[:, i:i + 1])
                    GG(v)
                    k.inc(v.tensor_scalar(out=Bx.nb1[:], in0=Bx.nb1[:], scalar1=-1.0,
                                          scalar2=None, op0=OP.mult), "ro", 1)
                else:
                    k.inc(v.engine_nop(), "ln", 1)
            block.vector(vn)

            if i < NTO:
                def sl(s, i=i, par=par):
                    k.w(s, "ro")
                    k.inc(s.activation(out=Bx.xnb[0][:], in_=Bx.xin[par][:],
                                       func=AF.Identity, bias=Bx.nb1[:],
                                       scale=Bx.r_tm[:, i:i + 1]), "exp", 1)
                block.scalar(sl)

                def vr(v, i=i):
                    k.w(v, "exp")
                    v.tensor_mul(out=Bx.xnb[1][:], in0=Bx.xnb[0][:], in1=g1bc[:])
                    v.reduce_max(out=Bx.rm1[:, i:i + 1], in_=Bx.xnb[1][:], axis=AX,
                                 apply_absolute_value=True)
                    v.tensor_mul(out=Bx.xnb[1][:], in0=Bx.xnb[0][:], in1=g2bc[:])
                    v.reduce_max(out=Bx.rm2[:, i:i + 1], in_=Bx.xnb[1][:], axis=AX,
                                 apply_absolute_value=True)
                    k.inc(v.engine_nop(), "ln", 1)
                block.vector(vr)
        n_ln1 = k.cnt["ln"]

        def vt(v):
            k.w(v, "ln", n_ln1)
            ms = Bx.stsc[:, 8:16]
            v.tensor_mul(out=ms[:], in0=Bx.r_tm[:, 0:NTO], in1=Bx.r_tm[:, 0:NTO])
            GG(v)
            v.tensor_mul(out=ms[:], in0=ms[:], in1=Bx.var_tm[:, 0:NTO])
            GG(v)
            k.inc(v.tensor_scalar(out=ms[:], in0=ms[:], scalar1=EPS_RMS, scalar2=None,
                                  op0=OP.add), "b4", 1)
        block.vector(vt)

        def st(s):
            k.w(s, "b4")
            k.inc(s.activation(out=Bx.stsc[:, 16:24], in_=Bx.stsc[:, 8:16],
                               func=AF.Sqrt), "b4", 1)
        block.scalar(st)

        def vt2(v):
            k.w(v, "b4")
            ms = Bx.stsc[:, 8:16]
            rr = Bx.stat48[:, 0:8]
            t1 = Bx.stat48[:, 8:16]
            v.reciprocal(out=rr[:], in_=Bx.stsc[:, 16:24])
            GG(v)
            v.tensor_mul(out=t1[:], in0=rr[:], in1=rr[:])
            GG(v)
            v.tensor_mul(out=t1[:], in0=t1[:], in1=ms[:])
            GG(v)
            v.tensor_scalar(out=t1[:], in0=t1[:], scalar1=-0.5, scalar2=1.5,
                            op0=OP.mult, op1=OP.add)
            GG(v)
            v.tensor_mul(out=rr[:], in0=rr[:], in1=t1[:])
            GG(v)
            q8 = Bx.stat48[:, 16:24]
            p8 = Bx.stat48[:, 24:32]
            for rm, iscx, gidx in ((Bx.rm1, Bx.isc1, 32), (Bx.rm2, Bx.isc2, 40)):
                v.tensor_mul(out=t1[:], in0=rr[:], in1=rm[:, 0:NTO])
                GG(v)
                v.tensor_scalar(out=t1[:], in0=t1[:], scalar1=1e-4, scalar2=None,
                                op0=OP.max)
                GG(v)
                v.tensor_scalar(out=iscx[:], in0=t1[:], scalar1=C1_SIG,
                                scalar2=None, op0=OP.mult)
                GG(v)
                v.reciprocal(out=q8[:], in_=t1[:])
                GG(v)
                v.tensor_mul(out=p8[:], in0=q8[:], in1=t1[:])
                GG(v)
                v.tensor_scalar(out=p8[:], in0=p8[:], scalar1=-1.0, scalar2=2.0,
                                op0=OP.mult, op1=OP.add)
                GG(v)
                v.tensor_mul(out=q8[:], in0=q8[:], in1=p8[:])
                GG(v)
                v.tensor_mul(out=q8[:], in0=q8[:], in1=rr[:])
                GG(v)
                v.tensor_scalar(out=Bx.stat48[:, gidx:gidx + 8], in0=q8[:],
                                scalar1=127.0, scalar2=None, op0=OP.mult)
                GG(v)
            v.tensor_copy(out=Bx.stat48[:, 0:16], in_=Bx.m_tm[:])
            k.inc(v.tensor_copy(out=Bx.stat48[:, 16:32], in_=Bx.r_tm[:]), "b3", 1)
        block.vector(vt2)

        def tf(t):
            k.w(t, "b3")
            k.w(t, "b1", 1)
            k.w(t, "ev")
            k.inc(t.transpose(Bx.pb[0][0:48, 0:128], Bx.stat48[:], Bx.ident[:]),
                  "tr", 1)
        block.tensor(tf)

        def sf(s):
            k.w(s, "tr")
            k.inc(s.copy(out=Bx.flip48[:], in_=Bx.pb[0][0:48, 0:128]), "ev", 1)
        block.scalar(sf)
        n_ev_fl = k.cnt["ev"]

        fl = Bx.fl_hbm.rearrange("a b -> (a b)")

        def syf(sy):
            k.w(sy, "ev", n_ev_fl)
            k.inc(sy.dma_start(out=Bx.fl_hbm, in_=Bx.flip48[:]), "st", 16)
            k.w(sy, "st")
            k.inc(sy.dma_start(out=Bx.row_s[:],
                               in_=fl[0:2048].rearrange("(o x) -> o x", o=1)), "w", 16)
        block.sync(syf)
        n_w_m = k.cnt["w"]

        def twm(t):
            k.w(t, "w", n_w_m)
        block.tensor(twm)
        _bcast_row(k, block, Bx, Bx.row_s, Bx.m_bc, TA)
        n_mm_m = k.cnt["mm"]

        def syr(sy):
            k.w(sy, "mm", n_mm_m)
            k.inc(sy.dma_start(out=Bx.row_s[:],
                               in_=fl[2048:4096].rearrange("(o x) -> o x", o=1)), "w", 16)
        block.sync(syr)
        n_w_r = k.cnt["w"]

        def twr(t):
            k.w(t, "w", n_w_r)
        block.tensor(twr)
        _bcast_row(k, block, Bx, Bx.row_s, Bx.r_bc, TA)
        Bx.n_ev_p1 = k.cnt["ev"]
        Bx.n_ln_gdone = ln0 + 8


def _p2(k, block, Bx):
    nc = k.nc
    with contextlib.ExitStack() as es2:
        wk_sb = es2.enter_context(nc.sbuf_tensor("wk_sb", [128, NC * 512], F32))
        wv_sb = es2.enter_context(nc.sbuf_tensor("wv_sb", [128, NC * 512], F32))

        def syw(sy):
            sy.wait_ge(k.sems["ln"], max(0, Bx.n_ln_gdone))
            k.inc(sy.dma_start(out=wk_sb[:].rearrange("p (c n) -> p c n", c=NC),
                               in_=Bx.wk_d.rearrange("(c p) n -> p c n", p=128)),
                  "w2", 16)
            k.inc(sy.dma_start(out=wv_sb[:].rearrange("p (c n) -> p c n", c=NC),
                               in_=Bx.wv_d.rearrange("(c p) n -> p c n", p=128)),
                  "w2", 16)
        block.sync(syw)
        n_wkv = k.cnt["w2"]

        def vones(v):
            vv = Bx.v_sb[:].rearrange("p (t g c) -> p t g c", t=NTA, g=4)
            k.inc(v.memset(vv[:, :, :, 128:129], 1.0), "b1", 1)
        block.vector(vones)

        ln0 = k.cnt["ln"]
        mm0 = k.cnt["mm"]
        ev0 = k.cnt["ev"]
        ro0 = k.cnt["ro"]
        tr0 = k.cnt["tr"]
        kf4 = Bx.k_fm[:].rearrange("p (g ta) -> p g ta", g=4)
        vv4 = Bx.v_sb[:].rearrange("p (t g c) -> p t g c", t=NTA, g=4)

        for i in range(NTA):
            par = i % 2
            pn = "xe" if par == 0 else "xo"

            def syd(sy, i=i, par=par, pn=pn):
                if i >= 2:
                    k.w(sy, "ln", ln0 + i - 1)
                k.xld[par] += 1
                k.inc(sy.dma_start(out=Bx.xin[par][:].rearrange("p (c j) -> p c j", c=NC),
                                   in_=Bx.x_fm4[i].rearrange("c p j -> p c j")), pn, 16)
            block.sync(syd)

            def ve(v, i=i, par=par, pn=pn):
                v.wait_ge(k.sems[pn], 16 * k.xld[par])
                if i < 2:
                    v.wait_ge(k.sems["ev"], max(0, Bx.n_ev_p1))
                if i >= 2:
                    v.wait_ge(k.sems["mm"], max(0, mm0 + 2 * (i - 1)))
                _ln_apply_fm(k, v, Bx, i, par)
                k.inc(v.engine_nop(), "ln", 1)
            block.vector(ve)

            def te(t, i=i, par=par):
                if i == 0:
                    k.w(t, "w2", n_wkv)
                    k.w(t, "ev")
                k.w(t, "ln", ln0 + i + 1)
                if i >= 1:
                    t.wait_ge(k.sems["ev"], max(0, ev0 + 3 * (i - 1)) + 1)
                xt = Bx.xnb[par]
                for c in range(NC):
                    inst = t.matmul(Bx.pb[0][:, 0:512], xt[:, c * 128:(c + 1) * 128],
                                    wk_sb[:, c * 512:(c + 1) * 512],
                                    start=(c == 0), stop=(c == NC - 1))
                k.inc(inst, "mm", 1)
                if i >= 1:
                    t.wait_ge(k.sems["ev"], max(0, ev0 + 3 * (i - 1)) + 2)
                for c in range(NC):
                    inst = t.matmul(Bx.pb[1][:, 0:512], xt[:, c * 128:(c + 1) * 128],
                                    wv_sb[:, c * 512:(c + 1) * 512],
                                    start=(c == 0), stop=(c == NC - 1))
                k.inc(inst, "mm", 1)
                if i >= 1:
                    t.wait_ge(k.sems["ro"], max(0, ro0 + i))
                    if i >= 3:
                        t.wait_ge(k.sems["ev"], max(0, ev0 + 3 * (i - 2)) + 3)
                    for b4i in range(4):
                        inst = t.transpose(Bx.pb[2 + ((i - 1) % 2)][:, b4i * 128:(b4i + 1) * 128],
                                           Bx.ktmp[:, b4i * 128:(b4i + 1) * 128], Bx.ident[:])
                    k.inc(inst, "tr", 1)
            block.tensor(te)

            def se(s, i=i):
                s.wait_ge(k.sems["mm"], max(0, mm0 + 2 * i + 1))
                if i >= 1:
                    s.wait_ge(k.sems["tr"], max(0, tr0 + i))
                k.inc(s.copy(out=Bx.ktmp[:], in_=Bx.pb[0][:, 0:512]), "ev", 1)
                s.wait_ge(k.sems["mm"], max(0, mm0 + 2 * i + 2))
                k.inc(s.copy(out=vv4[:, i, :, 0:128],
                             in_=Bx.pb[1][:, 0:512].rearrange("p (g c) -> p g c", g=4)),
                      "ev", 1)
                if i >= 1:
                    k.inc(s.copy(out=kf4[:, :, (i - 1) * 128:i * 128],
                                 in_=Bx.pb[2 + ((i - 1) % 2)][:, 0:512]
                                 .rearrange("p (g c) -> p g c", g=4)), "ev", 1)
                else:
                    k.inc(s.copy(out=Bx.sc11[0:1, 15:16], in_=Bx.ones1[0:1, 0:1]),
                          "ev", 1)
            block.scalar(se)

            def vr(v, i=i):
                v.wait_ge(k.sems["ev"], max(0, ev0 + 3 * i + 1))
                _rope(k, v, Bx, i, Bx.ktmp)
                k.inc(v.engine_nop(), "ro", 1)
            block.vector(vr)

        def tt(t):
            t.wait_ge(k.sems["ro"], max(0, ro0 + NTA))
            t.wait_ge(k.sems["ev"], max(0, ev0 + 3 * (NTA - 2)) + 3)
            for b4i in range(4):
                inst = t.transpose(Bx.pb[2 + ((NTA - 1) % 2)][:, b4i * 128:(b4i + 1) * 128],
                                   Bx.ktmp[:, b4i * 128:(b4i + 1) * 128], Bx.ident[:])
            k.inc(inst, "tr", 1)
        block.tensor(tt)

        def st(s):
            k.w(s, "tr")
            k.inc(s.copy(out=kf4[:, :, (NTA - 1) * 128:NTA * 128],
                         in_=Bx.pb[2 + ((NTA - 1) % 2)][:, 0:512]
                         .rearrange("p (g c) -> p g c", g=4)), "ev", 1)
        block.scalar(st)


def _p3_qproj(k, block, Bx, Q):
    def syw(sy):
        if Q == 0:
            k.w(sy, "ev")
        k.inc(sy.dma_start(out=Bx.wqo[:].rearrange("p (c n) -> p c n", c=NC),
                           in_=Bx.wq_d[:, Q * 512:(Q + 1) * 512]
                           .rearrange("(c p) n -> p c n", p=128)), "w2", 16)
    block.sync(syw)
    n_wq = k.cnt["w2"]

    ln0 = k.cnt["ln"]
    mm0 = k.cnt["mm"]
    ev0 = k.cnt["ev"]
    ro0 = k.cnt["ro"]
    tr0 = k.cnt["tr"]
    n_acc_q = k.cnt["acc"]
    qf4 = Bx.q_fm[:].rearrange("p (hh t) -> p hh t", hh=4)

    for i in range(NTO):
        par = i % 2
        pn = "xe" if par == 0 else "xo"

        def syd(sy, i=i, par=par, pn=pn):
            if i >= 2:
                k.w(sy, "ln", ln0 + i - 1)
            k.xld[par] += 1
            k.inc(sy.dma_start(out=Bx.xin[par][:].rearrange("p (c j) -> p c j", c=NC),
                               in_=Bx.x_fm4[i].rearrange("c p j -> p c j")), pn, 16)
        block.sync(syd)

        def ve(v, i=i, par=par, pn=pn):
            v.wait_ge(k.sems[pn], 16 * k.xld[par])
            if i >= 2:
                v.wait_ge(k.sems["mm"], max(0, mm0 + i - 1))
            _ln_apply_fm(k, v, Bx, i, par)
            k.inc(v.engine_nop(), "ln", 1)
        block.vector(ve)

        def te(t, i=i, par=par):
            if i == 0:
                k.w(t, "w2", n_wq)
            k.w(t, "ln", ln0 + i + 1)
            if i >= 1:
                t.wait_ge(k.sems["ev"], max(0, ev0 + 2 * (i - 1)) + 1)
            xt = Bx.xnb[par]
            for c in range(NC):
                inst = t.matmul(Bx.pb[0][:, 0:512], xt[:, c * 128:(c + 1) * 128],
                                Bx.wqo[:, c * 512:(c + 1) * 512],
                                start=(c == 0), stop=(c == NC - 1))
            k.inc(inst, "mm", 1)
            if i >= 1:
                t.wait_ge(k.sems["ro"], max(0, ro0 + i))
                if i >= 3:
                    t.wait_ge(k.sems["ev"], max(0, ev0 + 2 * (i - 3)) + 4)
                for b4i in range(4):
                    inst = t.transpose(Bx.pb[2 + ((i - 1) % 2)][:, b4i * 128:(b4i + 1) * 128],
                                       Bx.ktmp[:, b4i * 128:(b4i + 1) * 128], Bx.ident[:])
                k.inc(inst, "tr", 1)
        block.tensor(te)

        def se(s, i=i):
            s.wait_ge(k.sems["mm"], max(0, mm0 + i + 1))
            if i == 0:
                s.wait_ge(k.sems["acc"], max(0, n_acc_q))
            if i >= 1:
                s.wait_ge(k.sems["tr"], max(0, tr0 + i))
            k.inc(s.copy(out=Bx.ktmp[:], in_=Bx.pb[0][:, 0:512]), "ev", 1)
            if i >= 1:
                k.inc(s.activation(out=qf4[:, :, (i - 1) * 128:i * 128],
                                   in_=Bx.pb[2 + ((i - 1) % 2)][:, 0:512]
                                   .rearrange("p (hh c) -> p hh c", hh=4),
                                   func=AF.Identity, scale=CA_ATT), "ev", 1)
            else:
                k.inc(s.copy(out=Bx.sc11[0:1, 15:16], in_=Bx.ones1[0:1, 0:1]),
                      "ev", 1)
        block.scalar(se)

        def vr(v, i=i):
            v.wait_ge(k.sems["ev"], max(0, ev0 + 2 * i + 1))
            _rope(k, v, Bx, i, Bx.ktmp)
            k.inc(v.engine_nop(), "ro", 1)
        block.vector(vr)

    def tt(t):
        t.wait_ge(k.sems["ro"], max(0, ro0 + NTO))
        t.wait_ge(k.sems["ev"], max(0, ev0 + 2 * (NTO - 3)) + 4)
        for b4i in range(4):
            inst = t.transpose(Bx.pb[2 + ((NTO - 1) % 2)][:, b4i * 128:(b4i + 1) * 128],
                               Bx.ktmp[:, b4i * 128:(b4i + 1) * 128], Bx.ident[:])
        k.inc(inst, "tr", 1)
    block.tensor(tt)

    def st(s):
        k.w(s, "tr")
        k.inc(s.activation(out=qf4[:, :, (NTO - 1) * 128:NTO * 128],
                           in_=Bx.pb[2 + ((NTO - 1) % 2)][:, 0:512]
                           .rearrange("p (hh c) -> p hh c", hh=4),
                           func=AF.Identity, scale=CA_ATT), "ev", 1)
    block.scalar(st)


def _p3_attn(k, block, Bx, Q, nhh=4, nqs=2):
    E0 = k.cnt["exp"]
    A0 = k.cnt["av"]
    V0 = k.cnt["avv"]
    TV0 = k.cnt["trv"]
    n_ev_q = k.cnt["ev"]
    kf4 = Bx.k_fm[:].rearrange("p (g ta) -> p g ta", g=4)
    vv4 = Bx.v_sb[:].rearrange("p (t g c) -> p t g c", t=NTA, g=4)
    of4 = Bx.oh_fm[:].rearrange("p (hh t) -> p hh t", hh=4)

    for hh in range(nhh):
        for qs in range(nqs):
            it = hh * 2 + qs

            def te(t, hh=hh, qs=qs, it=it):
                if it == 0:
                    t.wait_ge(k.sems["ev"], max(0, n_ev_q))
                t.wait_ge(k.sems["avv"], max(0, V0 + it))
                t.wait_ge(k.sems["trv"], max(0, TV0 + it))
                for kt in range(NTA):
                    t.wait_ge(k.sems["exp"], max(0, E0 + it * 16 + kt - 1))
                    k.inc(t.matmul(Bx.pb[kt % 2][:, 0:512],
                                   kf4[:, Q, kt * 128:(kt + 1) * 128],
                                   Bx.q_fm[:, hh * TO + qs * 512: hh * TO + (qs + 1) * 512],
                                   start=True, stop=True), "mm", 1)
                    if kt >= 2:
                        kt2 = kt - 2
                        for sub in range(4):
                            inst = t.matmul(Bx.pb[2 + sub][:, 0:129],
                                            Bx.ptile[kt2 % 2][:, sub * 128:(sub + 1) * 128],
                                            vv4[:, kt2, Q, 0:129],
                                            start=(kt2 == 0), stop=False)
                        k.inc(inst, "av", 1)
                for kt2 in (NTA - 2, NTA - 1):
                    t.wait_ge(k.sems["exp"], max(0, E0 + it * 16 + kt2 + 1))
                    for sub in range(4):
                        inst = t.matmul(Bx.pb[2 + sub][:, 0:129],
                                        Bx.ptile[kt2 % 2][:, sub * 128:(sub + 1) * 128],
                                        vv4[:, kt2, Q, 0:129],
                                        start=False, stop=(kt2 == NTA - 1))
                    k.inc(inst, "av", 1)
            block.tensor(te)
            mm_after_scores = k.cnt["mm"]

            def se(s, it=it):
                for kt in range(NTA):
                    s.wait_ge(k.sems["mm"], max(0, mm_after_scores - (NTA - 1 - kt)))
                    s.wait_ge(k.sems["av"], max(0, A0 + it * 16 + kt - 1))
                    k.inc(s.activation(out=Bx.ptile[kt % 2][:],
                                       in_=Bx.pb[kt % 2][:, 0:512],
                                       func=AF.Exp, scale=CB_ATT), "exp", 1)
            block.scalar(se)

            def va(v, it=it):
                v.wait_ge(k.sems["av"], max(0, A0 + (it + 1) * 16))
                for sub in range(4):
                    v.reciprocal(out=Bx.rcp4[:, sub:sub + 1],
                                 in_=Bx.pb[2 + sub][:, 128:129])
                for sub in range(4):
                    v.tensor_tensor(out=Bx.rn4[:, sub:sub + 1],
                                    in0=Bx.pb[2 + sub][:, 128:129],
                                    in1=Bx.rcp4[:, sub:sub + 1], op=OP.mult)
                for sub in range(4):
                    v.tensor_scalar(out=Bx.rn4[:, sub:sub + 1],
                                    in0=Bx.rn4[:, sub:sub + 1],
                                    scalar1=-1.0, scalar2=2.0,
                                    op0=OP.mult, op1=OP.add)
                for sub in range(4):
                    v.tensor_tensor(out=Bx.rcp4[:, sub:sub + 1],
                                    in0=Bx.rcp4[:, sub:sub + 1],
                                    in1=Bx.rn4[:, sub:sub + 1], op=OP.mult)
                for sub in range(4):
                    v.tensor_scalar(out=Bx.ohtmp[:, sub * 128:(sub + 1) * 128],
                                    in0=Bx.pb[2 + sub][:, 0:128],
                                    scalar1=Bx.rcp4[:, sub:sub + 1], scalar2=None,
                                    op0=OP.mult)
                k.inc(v.engine_nop(), "avv", 1)
            block.vector(va)

            def tt(t, it=it):
                t.wait_ge(k.sems["avv"], max(0, V0 + it + 1))
                for sub in range(4):
                    inst = t.transpose(Bx.pb[0][:, sub * 128:(sub + 1) * 128],
                                       Bx.ohtmp[:, sub * 128:(sub + 1) * 128], Bx.ident[:])
                k.inc(inst, "tr", 1)
            block.tensor(tt)

            def so(s, it=it, hh=hh, qs=qs):
                k.w(s, "tr")
                k.inc(s.copy(out=of4[:, hh, qs * 512:(qs + 1) * 512],
                             in_=Bx.pb[0][:, 0:512]), "trv", 1)
            block.scalar(so)


def _p3_wo(k, block, Bx, Q):
    stage = [Bx.ktmp, Bx.ohtmp]
    TV_end = k.cnt["trv"]
    mm0 = k.cnt["mm"]
    st0 = k.cnt["st"]
    acc0 = k.cnt["acc"]
    of4 = Bx.oh_fm[:].rearrange("p (hh t) -> p hh t", hh=4)
    ev_marks = {}

    for os in range(8):
        def syw(sy, os=os):
            if os == 0:
                k.w(sy, "trv", TV_end)
            else:
                k.w(sy, "mm", mm0 + os * 8)
            k.inc(sy.dma_start(out=Bx.wos[:].rearrange("p (c n) -> p c n", c=4),
                               in_=Bx.wo_d[Q * 512:(Q + 1) * 512,
                                           os * 256:(os + 1) * 256]
                               .rearrange("(c p) n -> p c n", p=128)), "w2", 16)
        block.sync(syw)
        n_wo = k.cnt["w2"]

        for qt in range(NTO):
            idx = os * 8 + qt
            par = qt % 2

            if Q == 0:
                def syx(sy, os=os, qt=qt, par=par, idx=idx):
                    pn = "xe" if par == 0 else "xo"
                    if idx >= 2:
                        sy.wait_ge(k.sems["ev"], max(0, ev_marks[idx - 2]))
                    k.xld[par] += 1
                    k.inc(sy.dma_start(out=Bx.xin[par][:, 0:256],
                                       in_=Bx.x_tm[qt * 128:(qt + 1) * 128,
                                                   os * 256:(os + 1) * 256]), pn, 16)
                block.sync(syx)

            def te(t, os=os, qt=qt, par=par, idx=idx, n_wo=n_wo):
                if qt == 0:
                    k.w(t, "w2", n_wo)
                    if os == 0:
                        k.w(t, "trv", TV_end)
                if idx >= 2:
                    t.wait_ge(k.sems["ev"], max(0, ev_marks[idx - 2]))
                for c in range(4):
                    inst = t.matmul(Bx.pb[par][:, 0:256],
                                    of4[:, c, qt * 128:(qt + 1) * 128],
                                    Bx.wos[:, c * 256:(c + 1) * 256],
                                    start=(c == 0), stop=(c == 3))
                k.inc(inst, "mm", 1)
            block.tensor(te)
            mm_here = k.cnt["mm"]

            if Q == 0:
                def vev(v, os=os, qt=qt, par=par, idx=idx, mm_here=mm_here):
                    pn = "xe" if par == 0 else "xo"
                    v.wait_ge(k.sems[pn], 16 * k.xld[par])
                    v.wait_ge(k.sems["mm"], max(0, mm_here))
                    if idx >= 2:
                        v.wait_ge(k.sems["st"], max(0, st0 + 16 * (idx - 1)))
                    k.inc(v.tensor_tensor(out=stage[par][:, 0:256],
                                          in0=Bx.pb[par][:, 0:256],
                                          in1=Bx.xin[par][:, 0:256], op=OP.add),
                          "ev", 1)
                block.vector(vev)
                ev_marks[idx] = k.cnt["ev"]

                def sys(sy, os=os, qt=qt, par=par):
                    k.w(sy, "ev")
                    k.inc(sy.dma_start(out=Bx.out_d[qt * 128:(qt + 1) * 128,
                                                    os * 256:(os + 1) * 256],
                                       in_=stage[par][:, 0:256]), "st", 16)
                block.sync(sys)
            else:
                def sev(s, os=os, qt=qt, par=par, idx=idx, mm_here=mm_here):
                    s.wait_ge(k.sems["mm"], max(0, mm_here))
                    if idx >= 2:
                        s.wait_ge(k.sems["acc"], max(0, acc0 + 16 * (idx - 1)))
                    else:
                        s.wait_ge(k.sems["acc"], max(0, acc0))
                        s.wait_ge(k.sems["st"], max(0, st0))
                    k.inc(s.copy(out=stage[par][:, 0:256],
                                 in_=Bx.pb[par][:, 0:256]), "ev", 1)
                block.scalar(sev)
                ev_marks[idx] = k.cnt["ev"]

                def gac(g, os=os, qt=qt, par=par, idx=idx):
                    k.w(g, "ev")
                    if Q == 1 and idx == 0:
                        k.w(g, "st")
                    if NOACC:
                        k.inc(g.dma_start(out=Bx.h_hbm[qt * 128:(qt + 1) * 128,
                                                       os * 256:(os + 1) * 256],
                                          in_=stage[par][:, 0:256]), "acc", 16)
                    else:
                        k.inc(g.dma_start(out=Bx.out_d[qt * 128:(qt + 1) * 128,
                                                       os * 256:(os + 1) * 256],
                                          in_=stage[par][:, 0:256], accum_op=OP.add),
                              "acc", 16)
                block.gpsimd(gac)

    def syq(sy):
        k.w(sy, "ev")
        if Q == 0:
            k.w(sy, "st")
        else:
            k.w(sy, "acc")
    block.sync(syq)


def _p5a(k, block, Bx):
    nc = k.nc
    with contextlib.ExitStack() as esa:
        wmn = [esa.enter_context(nc.sbuf_tensor(f"wmn{i}", [128, 4096], F32))
               for i in range(2)]
        wpart = esa.enter_context(nc.sbuf_tensor("wpart", [128, 128], F32))

        pieces = []
        for c in range(NC):
            pieces.append((Bx.w1t_d[c * 128:(c + 1) * 128, 0:4096], 2 * c, 4096))
            pieces.append((Bx.w1t_d[c * 128:(c + 1) * 128, 4096:8192], 2 * c + 1, 4096))
        for c in range(NC):
            pieces.append((Bx.w2t_d[c * 128:(c + 1) * 128, 0:4096], 32 + 2 * c, 4096))
            pieces.append((Bx.w2t_d[c * 128:(c + 1) * 128, 4096:8192], 32 + 2 * c + 1, 4096))
        for c in range(64):
            pieces.append((Bx.w3t_d[c * 128:(c + 1) * 128, :], 64 + c, 2048))

        wm0 = k.cnt["wm"]
        for n_i, (src, col, wd) in enumerate(pieces):
            par = n_i % 2

            def syd(sy, n_i=n_i, src=src, par=par, wd=wd):
                if n_i >= 2:
                    k.w(sy, "wm", wm0 + n_i - 1)
                k.inc(sy.dma_start(out=wmn[par][:, 0:wd], in_=src), "w3", 16)
            block.sync(syd)
            n_w3 = k.cnt["w3"]

            def ve(v, n_i=n_i, col=col, par=par, n_w3=n_w3, wd=wd):
                v.wait_ge(k.sems["w3"], max(0, n_w3))
                k.inc(v.tensor_reduce(out=wpart[:, col:col + 1],
                                      in_=wmn[par][:, 0:wd],
                                      axis=AX, op=OP.add, apply_absolute_value=True),
                      "wm", 1)
            block.vector(ve)
        n_wm = k.cnt["wm"]

        def vred(v):
            k.w(v, "wm", n_wm)
            v.tensor_reduce(out=Bx.stsc[:, 20:21], in_=wpart[:, 0:32], axis=AX, op=OP.add)
            v.tensor_reduce(out=Bx.stsc[:, 21:22], in_=wpart[:, 32:64], axis=AX, op=OP.add)
            v.tensor_reduce(out=Bx.stsc[:, 22:23], in_=wpart[:, 64:128], axis=AX, op=OP.add)
            k.inc(v.engine_nop(), "b1", 1)
        block.vector(vred)

        def tred(t):
            k.w(t, "b1")
            k.inc(t.matmul(Bx.pb[0][0:3, 0:1], Bx.stsc[:, 20:23], Bx.onesc[:],
                           start=True, stop=True), "mm", 1)
        block.tensor(tred)

        def sev2(s):
            k.w(s, "mm")
            k.inc(s.copy(out=Bx.flip48[0:3, 0:1], in_=Bx.pb[0][0:3, 0:1]), "ev", 1)
        block.scalar(sev2)

        def syb(sy):
            k.w(sy, "ev")
            k.inc(sy.dma_start(out=Bx.fl_hbm[0:3, 0:1], in_=Bx.flip48[0:3, 0:1],
                               allow_slow_non_contiguous=True), "st", 16)
            k.w(sy, "st")
            fl = Bx.fl_hbm.rearrange("a b -> (a b)")
            k.inc(sy.dma_start(out=Bx.sc11[0:1, 4:7],
                               in_=fl[0:3 * 128:128].rearrange("(o x) -> o x", o=1),
                               allow_slow_non_contiguous=True), "w", 16)
        block.sync(syb)
        n_ws = k.cnt["w"]

        def vth(v):
            k.w(v, "w", n_ws)
            for i in range(3):
                v.tensor_scalar(out=Bx.sc11[0:1, 4 + i:5 + i],
                                in0=Bx.sc11[0:1, 4 + i:5 + i],
                                scalar1=1.0 / (D * HF), scalar2=None, op0=OP.mult)
            k.inc(v.tensor_scalar(out=Bx.sc11[0:1, 2:3], in0=Bx.ones1[0:1, 0:1],
                                  scalar1=0.7320508075688772, scalar2=None,
                                  op0=OP.add), "b3", 1)
        block.vector(vth)

        def sln(s):
            k.w(s, "b3")
            k.inc(s.activation(out=Bx.sc11[0:1, 3:4], in_=Bx.sc11[0:1, 2:3],
                               func=AF.Ln), "b3", 1)
        block.scalar(sln)

        def vth2(v):
            k.w(v, "b3")
            v.tensor_scalar(out=Bx.a_sc[0:1, 0:3], in0=Bx.sc11[0:1, 4:7],
                            scalar1=Bx.sc11[0:1, 3:4],
                            scalar2=float(EPS_T * 0.5493061443340548),
                            op0=OP.mult, op1=OP.add)
            v.tensor_scalar(out=Bx.a_sc[0:1, 6:9], in0=Bx.sc11[0:1, 4:7],
                            scalar1=1.0, scalar2=None, op0=OP.add)
            k.inc(v.tensor_scalar(out=Bx.a_sc[0:1, 9:12], in0=Bx.sc11[0:1, 4:7],
                                  scalar1=-1.0, scalar2=1.0,
                                  op0=OP.mult, op1=OP.add), "b3", 1)
        block.vector(vth2)

        def sln2(s):
            k.w(s, "b3")
            s.activation(out=Bx.a_sc[0:1, 6:9], in_=Bx.a_sc[0:1, 6:9], func=AF.Ln)
            k.inc(s.activation(out=Bx.a_sc[0:1, 9:12], in_=Bx.a_sc[0:1, 9:12],
                               func=AF.Ln), "b3", 1)
        block.scalar(sln2)

        def vhalf(v):
            k.w(v, "b3")
            k.inc(v.scalar_tensor_tensor(out=Bx.a_sc[0:1, 3:6],
                                         in0=Bx.a_sc[0:1, 9:12], scalar=-1.0,
                                         in1=Bx.a_sc[0:1, 6:9],
                                         op0=OP.mult, op1=OP.add), "b3", 1)
        block.vector(vhalf)

        def tb(t):
            k.w(t, "b3")
            for i in range(3):
                k.inc(t.matmul(Bx.pb[0][:, i:i + 1], Bx.ones1[:],
                               Bx.a_sc[0:1, i:i + 1], start=True, stop=True), "mm", 1)
            for i in range(3):
                k.inc(t.matmul(Bx.pb[1][:, i:i + 1], Bx.ones1[:],
                               Bx.a_sc[0:1, 3 + i:4 + i], start=True, stop=True),
                      "mm", 1)
        block.tensor(tb)

        def stb(s):
            k.w(s, "mm")
            for i in range(3):
                k.inc(s.copy(out=Bx.t_bc[i][:], in_=Bx.pb[0][:, i:i + 1]), "ev", 1)
            for i in range(3):
                k.inc(s.copy(out=Bx.nt_bc[i][:], in_=Bx.pb[1][:, i:i + 1]), "ev", 1)
        block.scalar(stb)

        def vsig(v):
            k.w(v, "ev")
            for i, (iscx, sigx) in enumerate(((Bx.isc1, Bx.sig1), (Bx.isc2, Bx.sig2))):
                v.tensor_scalar(out=sigx[:], in0=iscx[:], scalar1=Bx.nt_bc[i][:],
                                scalar2=C2_SIG, op0=OP.mult, op1=OP.mult)
            v.tensor_copy(out=Bx.a3_bc[:], in_=Bx.nt_bc[2][:])
            for i in range(3):
                v.tensor_scalar(out=Bx.nt_bc[i][:], in0=Bx.t_bc[i][:], scalar1=-1.0,
                                scalar2=None, op0=OP.mult)
            k.inc(v.engine_nop(), "b4", 1)
        block.vector(vsig)


def _p5b(k, block, Bx):
    nc = k.nc
    with contextlib.ExitStack() as esb:
        gam1_bc = esb.enter_context(nc.sbuf_tensor("gam1_bc", [128, TO], F32))
        gam2_bc = esb.enter_context(nc.sbuf_tensor("gam2_bc", [128, TO], F32))

        fl = Bx.fl_hbm.rearrange("a b -> (a b)")

        def syg1(sy):
            k.w(sy, "mm")
            k.inc(sy.dma_start(out=Bx.row_s[:, 0:TO],
                               in_=fl[4096:5120].rearrange("(o x) -> o x", o=1)),
                  "w", 16)
        block.sync(syg1)
        n_wgm1 = k.cnt["w"]

        def tw(t):
            k.w(t, "b4")
            k.w(t, "w", n_wgm1)
        block.tensor(tw)
        _bcast_row(k, block, Bx, Bx.row_s, gam1_bc, TO)
        n_mm_b1 = k.cnt["mm"]

        def syg2(sy):
            k.w(sy, "mm", n_mm_b1)
            k.inc(sy.dma_start(out=Bx.row_s[:, 0:TO],
                               in_=fl[5120:6144].rearrange("(o x) -> o x", o=1)),
                  "w", 16)
        block.sync(syg2)
        n_wgm2 = k.cnt["w"]

        def tw2(t):
            k.w(t, "w", n_wgm2)
        block.tensor(tw2)
        _bcast_row(k, block, Bx, Bx.row_s, gam2_bc, TO)
        n_evb = k.cnt["ev"]

        pe0 = k.cnt["pe"]
        lad0 = k.cnt["lad"]
        ln0 = k.cnt["ln"]
        n_inst = 0
        for c in range(NC):
            par = c % 2
            pn = "xe" if par == 0 else "xo"

            def syd(sy, c=c, par=par, pn=pn):
                if c >= 2:
                    k.w(sy, "ln", ln0 + c - 1)
                k.xld[par] += 1
                k.inc(sy.dma_start(out=Bx.xin[par][:, 0:1024]
                                   .rearrange("p (t j) -> p t j", t=NTO),
                                   in_=Bx.x_fm4[0:NTO, c].rearrange("t p j -> p t j")), pn, 16)
            block.sync(syd)

            def vln(v, c=c, par=par, pn=pn):
                if c == 0:
                    v.wait_ge(k.sems["ev"], max(0, n_evb))
                v.wait_ge(k.sems[pn], 16 * k.xld[par])
                xs = Bx.xin[par][:, 0:1024]
                xo = Bx.xnb[par][:, 0:1024]
                v.tensor_tensor(out=xo, in0=xs, in1=Bx.m_bc[:, 0:1024], op=OP.subtract)
                v.tensor_tensor(out=xo, in0=xo, in1=Bx.r_bc[:, 0:1024], op=OP.mult)
                k.inc(v.engine_nop(), "ln", 1)
            block.vector(vln)

            for gf, gbc, kdst in ((Bx.g1f, gam1_bc, Bx.k1), (Bx.g2f, gam2_bc, Bx.k2)):
                lp = n_inst % 2

                def yem(v, c=c, par=par, gf=gf, gbc=gbc, lp=lp):
                    k.w(v, "ln", ln0 + c + 1)
                    v.scalar_tensor_tensor(out=Bx.lz[lp][:, 0:1024],
                                           in0=Bx.xnb[par][:, 0:1024],
                                           scalar=gf[:, c:c + 1], in1=gbc[:],
                                           op0=OP.mult, op1=OP.mult)

                def dem(v, c=c, kdst=kdst, lp=lp):
                    v.tensor_scalar(out=kdst[:, c * 1024:(c + 1) * 1024],
                                    in0=Bx.lb[lp][:, 0:1024], scalar1=127.0,
                                    scalar2=None, op0=OP.min)

                _ladder_emit(k, block, Bx, n_inst, lp, 1024, yem, dem, pe0, lad0)
                n_inst += 1


def _p5c(k, block, Bx):
    nc = k.nc
    with contextlib.ExitStack() as esc:
        wbuf = esc.enter_context(nc.sbuf_tensor("wbuf", [128, 8 * 512], F32))
        n1 = esc.enter_context(nc.sbuf_tensor("n1_sb", [128, NC * 512], BF16))
        n2 = esc.enter_context(nc.sbuf_tensor("n2_sb", [128, NC * 512], BF16))
        g3bc = esc.enter_context(nc.sbuf_tensor("g3bc", [128, 512], F32))
        hst = [esc.enter_context(nc.sbuf_tensor(f"hst{i}", [128, 512], F32))
               for i in range(2)]
        su_t = esc.enter_context(nc.sbuf_tensor("su_t", [128, 512], F32))
        sq_t = esc.enter_context(nc.sbuf_tensor("sq_t", [128, 512], F32))

        tern0 = k.cnt["ro"]
        h0 = k.cnt["h"]
        ss_marks = {}
        vh_marks = {}
        sg_marks = {}
        sg_mmm = {}
        umm_marks = {}
        pe_all = k.cnt["pe"]

        for os in range(16):
            for half, (wt_d, nn, tix) in enumerate(
                    [(Bx.w1t_d, n1, 0), (None, None, None),
                     (Bx.w2t_d, n2, 1), (None, None, None)]):
                if wt_d is None:
                    continue
                for hp in range(2):
                    t_i = 4 * os + 2 * half // 2 + 0
                    def sywx(sy, os=os, wt_d=wt_d, hp=hp, half=half):
                        if os == 0 and half == 0 and hp == 0:
                            k.w(sy, "pe", pe_all)
                        k.w(sy, "ro", k.cnt["ro"])
                        if os >= 1 and half == 0 and hp == 0:
                            pass
                        k.inc(sy.dma_start(
                            out=wbuf[:].rearrange("p (c n) -> p c n", c=8),
                            in_=wt_d[hp * 1024:(hp + 1) * 1024,
                                     os * 512:(os + 1) * 512]
                            .rearrange("(c p) n -> p c n", p=128)), "w3", 16)
                    block.sync(sywx)
                    nwx = k.cnt["w3"]

                    def vtx(v, os=os, hp=hp, nn=nn, tix=tix, nwx=nwx):
                        v.wait_ge(k.sems["w3"], max(0, nwx))
                        if os >= 1 and hp == 0 and tix == 0:
                            v.wait_ge(k.sems["mm"], max(0, umm_marks[(os - 1, NTO - 1)]))
                        dst = nn[:, hp * 8 * 512:(hp + 1) * 8 * 512]
                        v.tensor_scalar(out=dst, in0=wbuf[:], scalar1=Bx.t_bc[tix][:],
                                        scalar2=None, op0=OP.is_gt)
                        k.inc(v.scalar_tensor_tensor(out=dst, in0=wbuf[:],
                                                     scalar=Bx.nt_bc[tix][:], in1=dst,
                                                     op0=OP.is_lt, op1=OP.subtract),
                              "ro", 1)
                    block.vector(vtx)

            def syg3(sy, os=os):
                if os >= 1:
                    k.w(sy, "mm", sg_mmm[os - 1])
                k.inc(sy.dma_start(out=Bx.g3s[:],
                                   in_=Bx.g3r_d[0:1, os * 512:(os + 1) * 512]),
                      "c", 16)
            block.sync(syg3)
            nc3 = k.cnt["c"]

            def tg(t, os=os, nc3=nc3):
                t.wait_ge(k.sems["c"], max(0, nc3))
                if os >= 1:
                    t.wait_ge(k.sems["ev"], max(0, sg_marks[os - 1] + 0))
                    t.wait_ge(k.sems["av"], max(0, k.cnt["av"]))
                k.inc(t.matmul(Bx.pb[4][:, 0:512], Bx.ones1[:], Bx.g3s[:],
                               start=True, stop=True), "mm", 1)
            block.tensor(tg)
            mm_g = k.cnt["mm"]
            sg_mmm[os] = mm_g

            def sg(s, os=os, mm_g=mm_g):
                s.wait_ge(k.sems["mm"], max(0, mm_g))
                k.inc(s.copy(out=g3bc[:], in_=Bx.pb[4][:, 0:512]), "ev", 1)
            block.scalar(sg)
            sg_marks[os] = k.cnt["ev"]

            for tt in range(NTO):
                par = tt % 2
                idx = os * 8 + tt

                def tu(t, os=os, tt=tt, par=par, idx=idx):
                    if tt == 0:
                        k.w(t, "ro", tern0 + 4 * os + 4)
                    if idx >= 2:
                        t.wait_ge(k.sems["ev"], max(0, ss_marks[idx - 2]))
                        t.wait_ge(k.sems["ev"], max(0, vh_marks[idx - 2]))
                    for c in range(NC):
                        inst = t.matmul(Bx.pb[par][:, 0:512],
                                        Bx.k1[:, c * TO + tt * 128: c * TO + (tt + 1) * 128],
                                        n1[:, c * 512:(c + 1) * 512],
                                        start=(c == 0), stop=(c == NC - 1))
                    k.inc(inst, "mm", 1)
                    for c in range(NC):
                        inst = t.matmul(Bx.pb[2 + par][:, 0:512],
                                        Bx.k2[:, c * TO + tt * 128: c * TO + (tt + 1) * 128],
                                        n2[:, c * 512:(c + 1) * 512],
                                        start=(c == 0), stop=(c == NC - 1))
                    k.inc(inst, "mm", 1)
                block.tensor(tu)
                umm_marks[(os, tt)] = k.cnt["mm"]
                mm_u = k.cnt["mm"]

                def ss(s, os=os, tt=tt, par=par, mm_u=mm_u):
                    s.wait_ge(k.sems["mm"], max(0, mm_u - 1))
                    k.inc(s.activation(out=su_t[:], in_=Bx.pb[par][:, 0:512],
                                       func=AF.Silu, scale=Bx.sig1[:, tt:tt + 1]),
                          "ev", 1)
                block.scalar(ss)
                ss_marks[idx] = k.cnt["ev"]
                ev_su = k.cnt["ev"]

                def vh(v, os=os, tt=tt, par=par, mm_u=mm_u, ev_su=ev_su, idx=idx):
                    v.wait_ge(k.sems["mm"], max(0, mm_u))
                    v.wait_ge(k.sems["ev"], max(0, ev_su))
                    if idx >= 2:
                        v.wait_ge(k.sems["h"], max(0, h0 + 16 * (idx - 1)))
                    k.inc(v.scalar_tensor_tensor(out=hst[par][:],
                                                 in0=Bx.pb[2 + par][:, 0:512],
                                                 scalar=Bx.sig2[:, tt:tt + 1],
                                                 in1=su_t[:], op0=OP.mult, op1=OP.mult),
                          "ev", 1)
                block.vector(vh)
                vh_marks[idx] = k.cnt["ev"]
                ev_h = k.cnt["ev"]

                def syh(sy, os=os, tt=tt, par=par, ev_h=ev_h):
                    k.w(sy, "ev", ev_h)
                    k.inc(sy.dma_start(out=Bx.h_hbm[tt * 128:(tt + 1) * 128,
                                                    os * 512:(os + 1) * 512],
                                       in_=hst[par][:]), "h", 16)
                block.sync(syh)

                def sqe(s, os=os, tt=tt, par=par, ev_h=ev_h):
                    k.w(s, "ev", ev_h)
                    k.inc(s.activation(out=sq_t[:], in_=hst[par][:], func=AF.Square,
                                       accum_out=Bx.ssq3[:, tt * 16 + os:
                                                         tt * 16 + os + 1]), "ev", 1)
                block.scalar(sqe)

                def vm(v, os=os, tt=tt, par=par):
                    k.w(v, "ev")
                    v.tensor_tensor(out=sq_t[:], in0=hst[par][:], in1=g3bc[:],
                                    op=OP.mult)
                    k.inc(v.reduce_max(out=Bx.rm3c[:, tt * 16 + os: tt * 16 + os + 1],
                                       in_=sq_t[:], axis=AX,
                                       apply_absolute_value=True), "av", 1)
                block.vector(vm)


def _p5d(k, block, Bx):
    def vt(v):
        k.w(v, "av")
        k.w(v, "ev")
        ms = Bx.stsc[:, 0:8]
        for tt in range(NTO):
            v.tensor_reduce(out=ms[:, tt:tt + 1],
                            in_=Bx.ssq3[:, tt * 16:(tt + 1) * 16], axis=AX, op=OP.add)
            v.tensor_reduce(out=Bx.stsc[:, 8 + tt:9 + tt],
                            in_=Bx.rm3c[:, tt * 16:(tt + 1) * 16], axis=AX, op=OP.max)
        GG(v)
        v.tensor_scalar(out=ms[:], in0=ms[:], scalar1=1.0 / HF, scalar2=EPS_RMS,
                        op0=OP.mult, op1=OP.add)
        k.inc(v.engine_nop(), "b4", 1)
    block.vector(vt)

    def st(s):
        k.w(s, "b4")
        k.inc(s.activation(out=Bx.stsc[:, 16:24], in_=Bx.stsc[:, 0:8], func=AF.Sqrt),
              "b4", 1)
    block.scalar(st)

    def vt2(v):
        k.w(v, "b4")
        ms = Bx.stsc[:, 0:8]
        rmx = Bx.stsc[:, 8:16]
        rr = Bx.gam3
        t1 = Bx.sig3
        r2 = Bx.stsc[:, 16:24]
        v.reciprocal(out=rr[:], in_=r2[:])
        GG(v)
        v.tensor_mul(out=t1[:], in0=rr[:], in1=rr[:])
        GG(v)
        v.tensor_mul(out=t1[:], in0=t1[:], in1=ms[:])
        GG(v)
        v.tensor_scalar(out=t1[:], in0=t1[:], scalar1=-0.5, scalar2=1.5,
                        op0=OP.mult, op1=OP.add)
        GG(v)
        v.tensor_mul(out=rr[:], in0=rr[:], in1=t1[:])
        GG(v)
        v.tensor_mul(out=t1[:], in0=rr[:], in1=rmx[:])
        GG(v)
        v.tensor_scalar(out=t1[:], in0=t1[:], scalar1=1e-4, scalar2=None, op0=OP.max)
        GG(v)
        v.reciprocal(out=r2[:], in_=t1[:])
        GG(v)
        v.tensor_mul(out=ms[:], in0=r2[:], in1=t1[:])
        GG(v)
        v.tensor_scalar(out=ms[:], in0=ms[:], scalar1=-1.0, scalar2=2.0,
                        op0=OP.mult, op1=OP.add)
        GG(v)
        v.tensor_mul(out=r2[:], in0=r2[:], in1=ms[:])
        GG(v)
        v.tensor_mul(out=r2[:], in0=r2[:], in1=rr[:])
        GG(v)
        v.tensor_scalar(out=Bx.gam3[:], in0=r2[:], scalar1=127.0, scalar2=None,
                        op0=OP.mult)
        v.tensor_scalar(out=t1[:], in0=t1[:], scalar1=C1_SIG, scalar2=None,
                        op0=OP.mult)
        GG(v)
        k.inc(v.tensor_scalar(out=Bx.sig3[:], in0=t1[:], scalar1=Bx.a3_bc[:],
                              scalar2=C2_SIG, op0=OP.mult, op1=OP.mult), "b4", 1)
    block.vector(vt2)


def _p5e(k, block, Bx):
    n_h_all = k.cnt["h"]
    pe0 = k.cnt["pe"]
    lad0 = k.cnt["lad"]
    kf3 = Bx.k3[:].rearrange("p (o t) -> p o t", o=64)
    g_marks = {}
    n_i = 0

    g_mmm = {}
    for os in range(16):
        def syg3(sy, os=os):
            if os >= 1:
                k.w(sy, "mm", g_mmm[os - 1])
            k.inc(sy.dma_start(out=Bx.g3s[:],
                               in_=Bx.g3r_d[0:1, os * 512:(os + 1) * 512]), "c", 16)
        block.sync(syg3)
        nc3 = k.cnt["c"]

        def tg(t, os=os, nc3=nc3):
            t.wait_ge(k.sems["c"], max(0, nc3))
            if os == 0:
                k.w(t, "b4")
                k.w(t, "mm")
            else:
                t.wait_ge(k.sems["pe"], max(0, pe0 + (n_i - 1) * 10 + 1))
            k.inc(t.matmul(Bx.pb[4][:, 0:512], Bx.ones1[:], Bx.g3s[:],
                           start=True, stop=True), "mm", 1)
        block.tensor(tg)
        mmg = k.cnt["mm"]
        g_mmm[os] = mmg

        def sg(s, os=os, mmg=mmg):
            s.wait_ge(k.sems["mm"], max(0, mmg))
            k.inc(s.copy(out=Bx.g3bc2[:], in_=Bx.pb[4][:, 0:512]), "ev", 1)
        block.scalar(sg)
        g_marks[os] = k.cnt["ev"]

        for tt in range(NTO):
            par = tt % 2
            lp = n_i % 2

            def syd(sy, os=os, tt=tt, par=par, n_i=n_i):
                if n_i == 0:
                    k.w(sy, "h", n_h_all)
                    k.w(sy, "mm")
                if n_i >= 2:
                    sy.wait_ge(k.sems["pe"], max(0, pe0 + (n_i - 1) * 10))
                k.inc(sy.dma_start(out=Bx.hld[par][:],
                                   in_=Bx.h_hbm[tt * 128:(tt + 1) * 128,
                                                os * 512:(os + 1) * 512]), "w", 16)
            block.sync(syd)
            nwl = k.cnt["w"]

            def yem(v, os=os, tt=tt, par=par, nwl=nwl, lp=lp):
                v.wait_ge(k.sems["w"], max(0, nwl))
                v.wait_ge(k.sems["ev"], max(0, g_marks[os]))
                v.scalar_tensor_tensor(out=Bx.lz[lp][:, 0:512], in0=Bx.hld[par][:],
                                       scalar=Bx.gam3[:, tt:tt + 1], in1=Bx.g3bc2[:],
                                       op0=OP.mult, op1=OP.mult)

            def dem(v, lp=lp):
                v.tensor_scalar(out=Bx.k3t[:], in0=Bx.lb[lp][:, 0:512], scalar1=127.0,
                                scalar2=None, op0=OP.min)

            _ladder_emit(k, block, Bx, n_i, lp, 512, yem, dem, pe0, lad0)

            def tr3(t, n_i=n_i):
                t.wait_ge(k.sems["pe"], max(0, pe0 + (n_i + 1) * 10))
                t.wait_ge(k.sems["trv"], max(0, k.cnt["trv"]))
                pp = Bx.pbb[n_i % 2]
                for sub in range(4):
                    inst = t.transpose(pp[:, sub * 128:(sub + 1) * 128],
                                       Bx.k3t[:, sub * 128:(sub + 1) * 128], Bx.identb[:])
                k.inc(inst, "tr", 1)
            block.tensor(tr3)
            ntr = k.cnt["tr"]

            def se3(s, os=os, tt=tt, n_i=n_i, ntr=ntr):
                s.wait_ge(k.sems["tr"], max(0, ntr))
                pp = Bx.pbb[n_i % 2]
                k.inc(s.copy(out=kf3[:, os * 4:(os + 1) * 4, tt * 128:(tt + 1) * 128],
                             in_=pp[:, 0:512].rearrange("p (g c) -> p g c", g=4)),
                      "trv", 1)
            block.scalar(se3)
            n_i += 1

    Bx.n_k3ev = k.cnt["trv"]


def _p5f(k, block, Bx):
    nc = k.nc
    with contextlib.ExitStack() as esf:
        w3b0 = esf.enter_context(nc.sbuf_tensor("w3b0", [128, 16 * 256], F32))
        w3b = [w3b0, w3b0]
        n3 = esf.enter_context(nc.sbuf_tensor("n3_sb", [128, 64 * 256], BF16))
        stage = [esf.enter_context(nc.sbuf_tensor(f"stg5{i}", [128, 256], F32))
                 for i in range(2)]
        kf3 = Bx.k3[:].rearrange("p (o t) -> p o t", o=64)
        n_k3 = Bx.n_k3ev
        acc0 = k.cnt["acc"]
        q30 = k.cnt["q3"]
        ev_marks = {}

        for dg in range(8):
            for qtr in range(4):
                tidx = dg * 4 + qtr

                def syw(sy, dg=dg, qtr=qtr, tidx=tidx):
                    if tidx == 0:
                        k.w(sy, "pe")
                        k.w(sy, "trv", n_k3)
                    if tidx >= 1:
                        k.w(sy, "q3", q30 + tidx)
                    src = Bx.w3t_d[qtr * 2048:(qtr + 1) * 2048,
                                   dg * 256:(dg + 1) * 256]
                    k.inc(sy.dma_start(out=w3b[qtr % 2][:]
                                       .rearrange("p (c n) -> p c n", c=16),
                                       in_=src.rearrange("(c p) n -> p c n", p=128)),
                          "w3", 16)
                block.sync(syw)
                nw = k.cnt["w3"]

                def vt(v, dg=dg, qtr=qtr, nw=nw):
                    v.wait_ge(k.sems["w3"], max(0, nw))
                    if dg >= 1 and qtr == 0:
                        v.wait_ge(k.sems["mm"], max(0, k.cnt["mm"]))
                    dst = n3[:, qtr * 16 * 256:(qtr + 1) * 16 * 256]
                    v.tensor_scalar(out=dst, in0=w3b[qtr % 2][:], scalar1=Bx.t_bc[2][:],
                                    scalar2=None, op0=OP.is_gt)
                    k.inc(v.scalar_tensor_tensor(out=dst, in0=w3b[qtr % 2][:],
                                                 scalar=Bx.nt_bc[2][:], in1=dst,
                                                 op0=OP.is_lt, op1=OP.subtract),
                          "q3", 1)
                block.vector(vt)

            n_tern = k.cnt["q3"]
            for tt in range(NTO):
                par = tt % 2
                idx = dg * 8 + tt

                def tm(t, dg=dg, tt=tt, par=par, idx=idx, n_tern=n_tern):
                    if tt == 0:
                        k.w(t, "q3", n_tern)
                        if dg == 0:
                            k.w(t, "trv", n_k3)
                    if idx >= 2:
                        t.wait_ge(k.sems["ev"], max(0, ev_marks[idx - 2]))
                    for oc in range(64):
                        inst = t.matmul(Bx.pb[par][:, 0:256],
                                        kf3[:, oc, tt * 128:(tt + 1) * 128],
                                        n3[:, oc * 256:(oc + 1) * 256],
                                        start=(oc == 0), stop=(oc == 63))
                    k.inc(inst, "mm", 1)
                block.tensor(tm)
                nmm = k.cnt["mm"]

                def sv(s, dg=dg, tt=tt, par=par, idx=idx, nmm=nmm):
                    s.wait_ge(k.sems["mm"], max(0, nmm))
                    if idx >= 2:
                        s.wait_ge(k.sems["acc"], max(0, acc0 + 16 * (idx - 1)))
                    else:
                        s.wait_ge(k.sems["acc"], max(0, acc0))
                    k.inc(s.activation(out=stage[par][:, 0:256],
                                       in_=Bx.pb[par][:, 0:256], func=AF.Identity,
                                       scale=Bx.sig3[:, tt:tt + 1]), "ev", 1)
                block.scalar(sv)
                ev_marks[idx] = k.cnt["ev"]
                nev = k.cnt["ev"]

                def ga(g, dg=dg, tt=tt, par=par, nev=nev):
                    g.wait_ge(k.sems["ev"], max(0, nev))
                    k.inc(g.dma_start(out=Bx.out_d[tt * 128:(tt + 1) * 128,
                                                   dg * 256:(dg + 1) * 256],
                                      in_=stage[par][:, 0:256], accum_op=OP.add),
                          "acc", 16)
                block.gpsimd(ga)


def kernel(L0_attn_hidden_states, L0_attn_Wq, L0_attn_Wk, L0_attn_Wv, L0_attn_Wo,
           L0_ffn_w1, L0_ffn_g1, L0_ffn_w2, L0_ffn_g2, L0_ffn_w3, L0_ffn_g3):
    x = np.ascontiguousarray(np.asarray(L0_attn_hidden_states, dtype=np.float32))
    wq = np.ascontiguousarray(np.asarray(L0_attn_Wq, dtype=np.float32))
    wk = np.ascontiguousarray(np.asarray(L0_attn_Wk, dtype=np.float32))
    wv = np.ascontiguousarray(np.asarray(L0_attn_Wv, dtype=np.float32))
    wo = np.ascontiguousarray(np.asarray(L0_attn_Wo, dtype=np.float32))
    w1t = np.ascontiguousarray(np.asarray(L0_ffn_w1, dtype=np.float32).T)
    w2t = np.ascontiguousarray(np.asarray(L0_ffn_w2, dtype=np.float32).T)
    w3t = np.ascontiguousarray(np.asarray(L0_ffn_w3, dtype=np.float32).T)
    g1 = np.asarray(L0_ffn_g1, dtype=np.float32)
    g2 = np.asarray(L0_ffn_g2, dtype=np.float32)
    g3 = np.asarray(L0_ffn_g3, dtype=np.float32)
    g1f = np.ascontiguousarray(g1.reshape(NC, 128).T)
    g2f = np.ascontiguousarray(g2.reshape(NC, 128).T)
    g1r = np.ascontiguousarray(g1.reshape(1, D))
    g2r = np.ascontiguousarray(g2.reshape(1, D))
    g3r = np.ascontiguousarray(g3.reshape(1, HF))

    nc = build()
    in_maps = []
    for core in range(8):
        b = core // 2
        half = core % 2
        own = x[b, half * TO:(half + 1) * TO, :]
        other = x[b, (1 - half) * TO:(2 - half) * TO, :]
        x_local = np.ascontiguousarray(np.concatenate([own, other], axis=0))
        x_fm4 = np.ascontiguousarray(
            x_local.reshape(NTA, 128, NC, 128).transpose(0, 2, 3, 1))
        pos = np.concatenate(
            [np.arange(half * TO, (half + 1) * TO, dtype=np.int32),
             np.arange((1 - half) * TO, (2 - half) * TO, dtype=np.int32)])
        posf = np.ascontiguousarray(pos.reshape(NTA, 128).T)
        in_maps.append({
            "x_tm": x_local, "x_fm4": x_fm4,
            "wq": wq, "wk": wk, "wv": wv, "wo": wo,
            "w1t": w1t, "w2t": w2t, "w3t": w3t,
            "g1f": g1f, "g2f": g2f, "g1r": g1r, "g2r": g2r, "g3r": g3r,
            "posf": posf,
        })
    res = run_bass_kernel_spmd(nc, in_maps, core_ids=list(range(8)))
    out = np.empty((4, TA, D), dtype=np.float32)
    for core in range(8):
        b = core // 2
        half = core % 2
        out[b, half * TO:(half + 1) * TO, :] = res.results[core]["out"]
    return out
